# version 20
# baseline (speedup 1.0000x reference)
import numpy as np
import concourse.bass as bass
import concourse.mybir as mybir
from concourse.bass_utils import run_bass_kernel_spmd

F32 = mybir.dt.float32
BF16 = mybir.dt.bfloat16
I32 = mybir.dt.int32
AF = mybir.ActivationFunctionType
ALU = mybir.AluOpType
AX = mybir.AxisListType

NCORES = 8
SEQS = 4
T = 2048
D = 1024
H = 16
NT = 16
NG = 8
F = 4096
FT = 32
EPS_A = 1.1920929e-07
EPS_L = 1e-5
NEG = -1e30
TCH = 256
NCH = T // TCH

SEM_OWNER = {"pe": "tensor", "dve": "vector", "act": "scalar", "pool": "gpsimd"}


class Sched:
    ENGS = ("sync", "tensor", "vector", "scalar", "gpsimd")

    def __init__(self):
        self.ops = {e: [] for e in self.ENGS}
        self.cnt = {}

    def op(self, eng, fn, waits=(), inc=None, amt=1):
        tok = None
        if inc is not None:
            self.cnt[inc] = self.cnt.get(inc, 0) + amt
            tok = (inc, self.cnt[inc])
        ws = tuple(w for w in waits if w is not None)
        self.ops[eng].append((ws, fn, inc, amt))
        return tok

    def replay(self, eng_name, eng, sems):
        hwm = {}
        rec = _Rec(eng)
        for waits, fn, inc, amt in self.ops[eng_name]:
            for s, v in waits:
                if hwm.get(s, -1) < v:
                    eng.wait_ge(sems[s], v)
                    hwm[s] = v
            rec.last = None
            fn(rec)
            if inc is not None:
                assert rec.last is not None, f"op on {eng_name} emitted nothing but incs {inc}"
                rec.last.then_inc(sems[inc], amt)


class _Rec:

    def __init__(self, e):
        self._e = e
        self.last = None

    def __getattr__(self, n):
        a = getattr(self._e, n)
        if callable(a):
            def g(*args, **kw):
                r = a(*args, **kw)
                if r is not None and hasattr(r, "then_inc"):
                    self.last = r
                return r
            return g
        return a


def cdiv(a, b):
    return (a + b - 1) // b


def build_program(seqs=SEQS, heads=H, do_attn=True, do_ffn=True):
    nc = bass.Bass("TRN2", target_bir_lowering=False, debug=False)

    x_hbm = nc.dram_tensor("x", [SEQS, T, D], F32, kind="ExternalInput").ap()
    W_DQ = nc.dram_tensor("W_DQ", [512, 64], F32, kind="ExternalInput").ap()
    W_UQ = nc.dram_tensor("W_UQ", [64, 512], F32, kind="ExternalInput").ap()
    W_DKV = nc.dram_tensor("W_DKV", [1024, 64], F32, kind="ExternalInput").ap()
    W_UKV = nc.dram_tensor("W_UKV", [128, 1024], F32, kind="ExternalInput").ap()
    W_QR = nc.dram_tensor("W_QR", [64, 64], F32, kind="ExternalInput").ap()
    W_KR = nc.dram_tensor("W_KR", [64, 64], F32, kind="ExternalInput").ap()
    W_O = nc.dram_tensor("W_O", [64, 64], F32, kind="ExternalInput").ap()
    qnw = nc.dram_tensor("qnw", [64], F32, kind="ExternalInput").ap()
    knw = nc.dram_tensor("knw", [64], F32, kind="ExternalInput").ap()
    gam = nc.dram_tensor("gam", [1024], F32, kind="ExternalInput").ap()
    bet = nc.dram_tensor("bet", [1024], F32, kind="ExternalInput").ap()
    q1d = nc.dram_tensor("q1", [1024, 4096], I32, kind="ExternalInput").ap()
    s1d = nc.dram_tensor("s1", [8, 4096], F32, kind="ExternalInput").ap()
    b1d = nc.dram_tensor("b1", [4096], F32, kind="ExternalInput").ap()
    q2d = nc.dram_tensor("q2", [4096, 1024], I32, kind="ExternalInput").ap()
    s2d = nc.dram_tensor("s2", [32, 1024], F32, kind="ExternalInput").ap()
    b2d = nc.dram_tensor("b2", [1024], F32, kind="ExternalInput").ap()
    out_hbm = nc.dram_tensor("out", [SEQS, T, D], F32, kind="ExternalOutput").ap()
    w2bf = nc.dram_tensor("w2bf", [F, D], BF16, kind="Internal").ap()
    w1hi = nc.dram_tensor("w1hi", [FT, 4, 128, 128], BF16, kind="Internal").ap()

    sch = Sched()
    guards = []

    def sb(name, shape, dtype):
        g = nc.sbuf_tensor(name, shape, dtype)
        h = g.__enter__()
        guards.append(g)
        return h

    def psum(name, shape, dtype):
        g = nc.psum_tensor(name, shape, dtype)
        h = g.__enter__()
        guards.append(g)
        return h

    ident_b = sb("ident_b", [128, 128], BF16)
    ident_f = sb("ident_f", [128, 128], F32)
    cmask = sb("cmask", [128, 128], F32)
    MQt = sb("MQt", [128, 64], BF16)
    Mkv = sb("Mkv", [128, 128], BF16)
    WQRt_s = sb("WQRt_s", [64, 64], BF16)
    WKRt_s = sb("WKRt_s", [128, 64], BF16)
    WOt_s = sb("WOt_s", [64, 64], BF16)
    wq_col = sb("wq_col", [64, 1], F32)
    wk_col = sb("wk_col", [64, 1], F32)
    ones64 = sb("ones64", [64, 1], BF16)
    ones1r = sb("ones1r", [65, 64], BF16)
    gcol = sb("gcol", [128, 8], F32)
    bcol = sb("bcol", [128, 8], F32)
    b1cols = sb("b1cols", [128, 32], F32)
    c_epsA = sb("c_epsA", [128, 1], F32)
    c_epsL = sb("c_epsL", [128, 1], F32)
    row32 = sb("row32", [65, 1024], F32)
    rstdk_r = sb("rstdk_r", [65, 2048], BF16)
    attn2 = sb("attn2", [128, 2048], BF16)
    statc = sb("statc", [128, 64], F32)
    statc_B = sb("statc_B", [128, 64], F32)
    attn3 = sb("attn3", [128, 2048], BF16)
    attn4 = sb("attn4", [128, 2048], BF16)
    statl = sb("statl", [128, 64], F32)
    stats2 = sb("stats2", [128, 48], F32)
    x_tok = sb("x_tok", [128, NT * 1024], F32)
    featT = sb("featT", [128, NG * 2048], BF16)
    w1_sb = sb("w1_sb", [128, 4 * 4096], BF16)
    whi = sb("whi", [128, 4 * 512], BF16)
    qT_B = sb("qT_B", [128, 2048], BF16)
    kT_B = sb("kT_B", [128, 2048], BF16)
    vtok_B = sb("vtok_B", [128, 1024], BF16)
    ctxT_Bt = sb("ctxT_Bt", [64, 2048], BF16)
    U = sb("U", [128, 16 * 1024 * 2 // 2], BF16)

    negm_c = statc.ap()[:, 0:16]
    l_c = statc.ap()[:, 16:32]
    rl_c = statc.ap()[:, 32:48]
    rstdq_c = statc.ap()[:, 48:64]
    negmu1_c = statl.ap()[:, 0:16]
    ss1_c = statl.ap()[:, 16:32]
    r1x2_c = statl.ap()[:, 32:48]
    rho_c = statl.ap()[:, 48:64]
    biasq_c = stats2.ap()[:, 0:16]
    tmpa_c = stats2.ap()[:, 16:32]
    tmpb_c = stats2.ap()[:, 32:48]

    Uf = U.bitcast(F32)
    qT = U.ap()[:, 0:2048]
    kT = U.ap()[:, 2048:4096]
    attn = U.ap()[:, 4096:6144]
    sqk = U.ap()[0:64, 4096:6144]
    attnT = U.ap()[:, 6144:8192]
    rstdb = U.ap()[0:64, 6144:8192]
    vtok = U.ap()[:, 8192:9216]
    ctxT = U.ap()[0:64, 9216:11264]
    vfm = U.ap()[0:64, 9216:11264]
    xcvt = U.ap()[:, 11264:12288]
    outFM = Uf.ap()[:, 6144:7168]
    outFM = Uf.ap()[:, 6144:8192]
    yT = U.ap()[:, 0:FT * TCH]
    w2db = U.ap()[:, 8192:14336]
    sqdump = U.ap()[:, 14336:15360]
    b2half = U.ap()[:, 15360:16384]
    xt_i = x_tok.bitcast(I32)
    p1_int = xt_i.ap()[:, 0:4096]
    p1_conv = x_tok.ap()[:, 4096:8192]
    p1_bc = x_tok.ap()[:, 8192:12288]
    b1_row = x_tok.ap()[0:1, 12288:16384]
    gam_row = x_tok.ap()[0:1, 8192:9216]
    bet_row = x_tok.ap()[0:1, 9216:10240]
    p2_int = xt_i.ap()[:, 0:1024]
    p2_int2 = xt_i.ap()[:, 1024:2048]
    p2_conv = x_tok.ap()[:, 4096:5120]
    p2_conv2 = x_tok.ap()[:, 5120:6144]
    p2_bc = x_tok.ap()[:, 6144:7168]
    p2_bc2 = x_tok.ap()[:, 7168:8192]
    fTf = featT.bitcast(F32)
    p2_bf = featT.ap()[:, 0:1024]
    p2_bf2 = featT.ap()[:, 1024:2048]
    wld = fTf.ap()[:, 2048:3072]
    wld2 = fTf.ap()[:, 3072:3584]
    wukv_ld = fTf.ap()[:, 3584:4608]
    wqr_f = fTf.ap()[0:64, 4608:4672]
    wkr_f = fTf.ap()[0:64, 4672:4736]
    wo_f = fTf.ap()[0:64, 4736:4800]
    wbfs = featT.ap()[:, 9600:10880]
    wuqT = featT.ap()[:, 10880:11136]
    wukvT = featT.ap()[:, 11136:12160]
    wqrT = featT.ap()[0:64, 12160:12224]
    wkrT = featT.ap()[0:64, 12224:12288]
    woT = featT.ap()[0:64, 12288:12352]

    R0 = psum("R0", [128, 1024], F32)
    R1 = psum("R1", [128, 1024], F32)
    R2 = psum("R2", [128, 1024], F32)
    TS = psum("TS", [128, 1024], BF16)
    CX = psum("CX", [128, 512], F32)
    REG = [R0, R1, R2]
    Bp = R2

    sems = {}
    for s in ("pe", "dve", "act", "pool", "dqs", "dq1", "dq2a", "dq2b",
              "gqa", "gqb", "dxl", "w2s0", "w2s1", "w2s2", "w2s3", "w2s4", "w2s5",
              "w1a", "w1b", "w1c", "w1d", "gqc", "oq", "gqb2", "oq2", "dxl2"):
        g = nc.semaphore(s)
        sems[s] = g.__enter__()
        guards.append(g)

    OP = sch.op

    def dma(eng, qsem, out, in_, waits=(), slow=False):
        def fn(e, out=out, in_=in_, slow=slow):
            if slow:
                return e.dma_start(out=out, in_=in_, allow_slow_non_contiguous=True)
            return e.dma_start(out=out, in_=in_)
        return OP(eng, fn, waits=waits, inc=qsem, amt=16)

    id64b = ident_b.ap()[0:64, 0:64]
    id128b = ident_b.ap()
    id1f = ident_f.ap()[0:1, 0:1]
    id128f = ident_f.ap()

    def v_consts(e):
        e.memset(c_epsA.ap(), EPS_A)
        e.memset(c_epsL.ap(), EPS_L)
        e.memset(ones64.ap(), 1.0)
        e.memset(ones1r.ap(), 1.0)
        e.memset(ident_b.ap(), 0.0)
        e.memset(ident_f.ap(), 0.0)
        return e.memset(cmask.ap(), 0.0)
    t_csts = OP("vector", v_consts, inc="dve")

    def g_setup(e):
        e.affine_select(out=ident_b.ap(), in_=ident_b.ap(),
                        compare_op=ALU.not_equal, fill=1.0, base=0,
                        pattern=[[-1, 128]], channel_multiplier=1)
        e.affine_select(out=ident_f.ap(), in_=ident_f.ap(),
                        compare_op=ALU.not_equal, fill=1.0, base=0,
                        pattern=[[-1, 128]], channel_multiplier=1)
        return e.affine_select(out=cmask.ap(), in_=cmask.ap(),
                               compare_op=ALU.is_ge, fill=NEG, base=0,
                               pattern=[[-1, 128]], channel_multiplier=1)
    t_setup = OP("gpsimd", g_setup, waits=[t_csts], inc="pool")

    t_ldq = dma("sync", "dqs", wld[:, 0:256].rearrange("p (c d) -> p c d", c=4),
                W_DQ.rearrange("(c p) d -> p c d", p=128))
    t_ldkv = dma("sync", "dqs", wld[:, 256:768].rearrange("p (c d) -> p c d", c=8),
                 W_DKV.rearrange("(c p) d -> p c d", p=128))
    t_luq = dma("sync", "dqs", wld2[0:64, 0:512], W_UQ)
    t_lukv = dma("sync", "dqs", wukv_ld, W_UKV)
    t_lqr = dma("sync", "dqs", wqr_f, W_QR)
    t_lkr = dma("sync", "dqs", wkr_f, W_KR)
    t_lwo = dma("sync", "dqs", wo_f, W_O)
    t_lqn = dma("sync", "dqs", wq_col.ap(), qnw.rearrange("(d o) -> d o", o=1), slow=True)
    t_lkn = dma("sync", "dqs", wk_col.ap(), knw.rearrange("(d o) -> d o", o=1), slow=True)
    t_lg = dma("sync", "dqs", gam_row, gam.rearrange("(o d) -> o d", o=1))
    t_lb = dma("sync", "dqs", bet_row, bet.rearrange("(o d) -> o d", o=1))
    t_lb1 = dma("sync", "dqs", b1_row, b1d.rearrange("(o d) -> o d", o=1))
    t_small = ("dqs", sch.cnt["dqs"])
    t_wq = OP("vector", lambda e: e.tensor_scalar_mul(out=wq_col.ap(), in0=wq_col.ap(),
                                                      scalar1=0.125),
              waits=[t_small], inc="dve")

    t_cdq = OP("vector", lambda e: e.tensor_copy(out=wbfs[:, 0:256], in_=wld[:, 0:256]),
               waits=[t_small], inc="dve")
    t_cdkv = OP("vector", lambda e: e.tensor_copy(out=wbfs[:, 256:768], in_=wld[:, 256:768]),
                waits=[t_small], inc="dve")
    t_cuq = OP("vector", lambda e: e.tensor_copy(out=wbfs[0:64, 768:1280],
                                                 in_=wld2[0:64, 0:512]),
               waits=[t_small], inc="dve")
    wukv_bf = wbfs
    wukv_bf = featT.ap()[:, 2048:3072]
    t_cukv = OP("vector", lambda e: e.tensor_copy(out=wukv_bf, in_=wukv_ld),
                waits=[t_small], inc="dve")
    wqr_b = featT.ap()[0:64, 3072:3136]
    wkr_b = featT.ap()[0:64, 3136:3200]
    wo_b = featT.ap()[0:64, 3200:3264]
    t_cqr = OP("vector", lambda e: e.tensor_copy(out=wqr_b, in_=wqr_f), waits=[t_small], inc="dve")
    t_ckr = OP("vector", lambda e: e.tensor_copy(out=wkr_b, in_=wkr_f), waits=[t_small], inc="dve")
    t_cwo = OP("vector", lambda e: e.tensor_copy(out=wo_b, in_=wo_f), waits=[t_small], inc="dve")

    def pe_tr_wuq(e):
        for c in range(4):
            e.transpose(TS.ap()[:, 64 * c:64 * (c + 1)],
                        wbfs[0:64, 768 + 128 * c:768 + 128 * (c + 1)], id64b)
        return None
    t_pe1 = OP("tensor", pe_tr_wuq, waits=[t_cuq, t_setup], inc="pe")
    t_ev1 = OP("vector", lambda e: e.tensor_copy(out=wuqT, in_=TS.ap()[:, 0:256]),
               waits=[t_pe1], inc="dve")

    def pe_tr_wukv(e):
        for c in range(8):
            e.transpose(TS.ap()[:, 128 * c:128 * (c + 1)],
                        wukv_bf[:, 128 * c:128 * (c + 1)], id128b)
        return None
    t_pe2 = OP("tensor", pe_tr_wukv, waits=[t_cukv, t_setup, t_ev1], inc="pe")
    t_ev2 = OP("vector", lambda e: e.tensor_copy(out=wukvT, in_=TS.ap()[:, 0:1024]),
               waits=[t_pe2], inc="dve")

    def pe_tr_small(e):
        e.transpose(TS.ap()[0:64, 0:64], wqr_b, id64b)
        e.transpose(TS.ap()[0:64, 64:128], wkr_b, id64b)
        e.transpose(TS.ap()[0:64, 128:192], wo_b, id64b)
        e.transpose(TS.ap()[64:128, 192:256], wkr_b, id64b, tile_position=(0, 64))
        return None
    t_pe3 = OP("tensor", pe_tr_small, waits=[t_cqr, t_ckr, t_cwo, t_ev2], inc="pe")

    def ev_small(e):
        e.tensor_copy(out=wqrT, in_=TS.ap()[0:64, 0:64])
        e.tensor_copy(out=wkrT, in_=TS.ap()[0:64, 64:128])
        e.tensor_copy(out=woT, in_=TS.ap()[0:64, 128:192])
        e.tensor_copy(out=WQRt_s.ap(), in_=TS.ap()[0:64, 0:64])
        e.tensor_copy(out=WOt_s.ap(), in_=TS.ap()[0:64, 128:192])
        e.tensor_copy(out=WKRt_s.ap()[0:64, :], in_=TS.ap()[0:64, 64:128])
        return e.tensor_copy(out=WKRt_s.ap()[64:128, :], in_=TS.ap()[64:128, 192:256])
    t_ev3 = OP("vector", ev_small, waits=[t_pe3], inc="dve")

    def pe_mqt(e):
        for tp in ((0, 0), (0, 64)):
            half = 0 if tp[1] == 0 else 1
            for c in range(4):
                e.matmul(CX.ap()[64 * half:64 * (half + 1), 0:64],
                         wbfs[:, 64 * c:64 * (c + 1)],
                         wuqT[:, 64 * c:64 * (c + 1)],
                         start=(c == 0), stop=(c == 3), tile_position=tp)
        return None
    t_mqt = OP("tensor", pe_mqt, waits=[t_cdq, t_ev1], inc="pe")
    t_emqt = OP("vector", lambda e: e.tensor_copy(out=MQt.ap(), in_=CX.ap()[:, 0:64]),
                waits=[t_mqt], inc="dve")

    def pe_mkv(e):
        for tp in ((0, 0), (0, 64)):
            half = 0 if tp[1] == 0 else 1
            for c in range(8):
                e.matmul(CX.ap()[64 * half:64 * (half + 1), 64:192],
                         wbfs[:, 256 + 64 * c:256 + 64 * (c + 1)],
                         wukvT[:, 128 * c:128 * (c + 1)],
                         start=(c == 0), stop=(c == 7), tile_position=tp)
        return None
    t_mkv = OP("tensor", pe_mkv, waits=[t_cdkv, t_ev2, t_emqt], inc="pe")
    t_emkv = OP("vector", lambda e: e.tensor_copy(out=Mkv.ap(), in_=CX.ap()[:, 64:192]),
                waits=[t_mkv], inc="dve")

    def pe_rows(e):
        for c in range(32):
            e.transpose(CX.ap()[:, 192 + c:193 + c], b1_row[:, 128 * c:128 * (c + 1)], id1f)
        for c in range(8):
            e.transpose(CX.ap()[:, 224 + c:225 + c], gam_row[:, 128 * c:128 * (c + 1)], id1f)
        for c in range(8):
            e.transpose(CX.ap()[:, 232 + c:233 + c], bet_row[:, 128 * c:128 * (c + 1)], id1f)
        return None
    t_rows = OP("tensor", pe_rows, waits=[t_small, t_setup, t_emkv], inc="pe")

    def ev_rows(e):
        e.tensor_copy(out=b1cols.ap(), in_=CX.ap()[:, 192:224])
        e.tensor_copy(out=gcol.ap(), in_=CX.ap()[:, 224:232])
        return e.tensor_copy(out=bcol.ap(), in_=CX.ap()[:, 232:240])
    t_evrows = OP("vector", ev_rows, waits=[t_rows], inc="dve")

    last_w1 = None
    last_w1hi_st = None
    t_cv_prev = None
    xt_b = x_tok.bitcast(BF16)
    w1st = xt_b.ap()[:, 24576:28672]
    for g in range(NG):
        wl = [last_w1] if last_w1 else [t_rows]
        if g == 4:
            wl.append(t_rows)
        ti = dma("sync", "dq1", p1_int, q1d[128 * g:128 * (g + 1), :], waits=wl)
        tb = dma("sync", "dq1", p1_bc, s1d[g:g + 1, :].broadcast_to([128, 4096]),
                 waits=wl + ([t_evrows] if g == 0 else []))
        t_pair = ("dq1", sch.cnt["dq1"])
        tc = OP("vector", lambda e: e.tensor_copy(out=p1_conv, in_=p1_int),
                waits=[t_pair, t_cv_prev], inc="dve")
        t_cv_prev = tc
        if g < 4:
            last_w1 = OP("vector",
                         lambda e, g=g: e.scalar_tensor_tensor(
                             out=w1_sb.ap()[:, 4096 * g:4096 * (g + 1)],
                             in0=p1_conv, scalar=-128.0, in1=p1_bc,
                             op0=ALU.add, op1=ALU.mult),
                         waits=[tc, t_pair], inc="dve")
        else:
            tdq = OP("vector",
                     lambda e: e.scalar_tensor_tensor(
                         out=w1st, in0=p1_conv, scalar=-128.0, in1=p1_bc,
                         op0=ALU.add, op1=ALU.mult),
                     waits=[tc, t_pair, last_w1hi_st], inc="dve")
            last_w1hi_st = dma("gpsimd", "gqc",
                               w1hi[:, g - 4].rearrange("f p c -> p f c"),
                               w1st.rearrange("p (f c) -> p f c", f=32),
                               waits=[tdq])
            last_w1 = tdq

    st_prev = [None, None]
    dq_prev = [None, None]
    for fc in range(FT):
        par = fc % 2
        ibuf = p2_int if par == 0 else p2_int2
        cbuf = p2_conv if par == 0 else p2_conv2
        bbuf = p2_bc if par == 0 else p2_bc2
        obuf = p2_bf if par == 0 else p2_bf2
        wl = [last_w1] if fc < 2 else []
        if st_prev[par]:
            wl.append(st_prev[par])
        dsem = "dq2a" if par == 0 else "dq2b"
        gsem = "gqa" if par == 0 else "gqb"
        ti = dma("sync", dsem, ibuf, q2d[128 * fc:128 * (fc + 1), :], waits=wl)
        tb = dma("sync", dsem, bbuf, s2d[fc:fc + 1, :].broadcast_to([128, 1024]),
                 waits=wl)
        t_pair2 = (dsem, sch.cnt[dsem])
        tc = OP("vector", lambda e, i=ibuf, c=cbuf: e.tensor_copy(out=c, in_=i),
                waits=[t_pair2, dq_prev[par]], inc="dve")
        td = OP("vector",
                lambda e, o=obuf, c=cbuf, b=bbuf: e.scalar_tensor_tensor(
                    out=o, in0=c, scalar=-128.0, in1=b, op0=ALU.add, op1=ALU.mult),
                waits=[tc], inc="dve")
        dq_prev[par] = td
        st_prev[par] = dma("gpsimd", gsem, w2bf[128 * fc:128 * (fc + 1), :], obuf,
                           waits=[td])

    prologue_done = [t for t in (last_w1, last_w1hi_st, st_prev[0], st_prev[1], t_emkv, t_ev3,
                                 t_evrows, t_csts, t_wq, t_small, t_setup) if t]

    n_store = 0
    prev_seq_done = None

    fc1_done_prev = [None] * 8
    half_store_toks = [None, None]

    for s in range(seqs):
        tok_xload = [None] * NT
        for half in range(2):
            hsem = "dxl" if half == 0 else "dxl2"
            xw = list(prologue_done) if s == 0 else [half_store_toks[half]]
            for t in range(8 * half, 8 * half + 8):
                dma("sync", hsem, x_tok.ap()[:, 1024 * t:1024 * (t + 1)],
                    x_hbm[s, 128 * t:128 * (t + 1), :], waits=xw)
                xw = []
            t_h = (hsem, sch.cnt[hsem])
            for t in range(8 * half, 8 * half + 8):
                tok_xload[t] = t_h

        t_ev_prev = None
        t_xcvt_reader = None
        for t in range(NT):
            w = [tok_xload[t], t_xcvt_reader]
            t_cv = OP("vector",
                      lambda e, t=t: e.tensor_copy(out=xcvt,
                                                   in_=x_tok.ap()[:, 1024 * t:1024 * (t + 1)]),
                      waits=w, inc="dve")
            def pe_xtr(e):
                for g2 in range(8):
                    e.transpose(TS.ap()[:, 128 * g2:128 * (g2 + 1)],
                                xcvt[:, 128 * g2:128 * (g2 + 1)], id128b)
                return None
            t_tr = OP("tensor", pe_xtr, waits=[t_cv, t_ev_prev], inc="pe")
            t_xcvt_reader = t_tr
            def ev_fn(e, t=t):
                return e.tensor_copy(
                    out=featT.ap().rearrange("p (g c) -> p g c", g=NG)[:, :, 128 * t:128 * (t + 1)],
                    in_=TS.ap().rearrange("p (g c) -> p g c", g=8))
            t_ev_prev = OP("vector", ev_fn,
                           waits=[t_tr, fc1_done_prev[t // 2]], inc="dve")
        featT_all = [t_ev_prev]

        tok_xtok_write = [None] * NT
        last_head_tokens = featT_all
        ts_reader = featT_all[0]

        npairs = (heads if do_attn else 0) // 2
        g_reg_free = [[featT_all[0]], [featT_all[0]], [featT_all[0]]]
        g_si = [0]
        g_ev_prev = [[ts_reader]]
        g_attnT_rd = [None]
        g_cev = [None]

        def mk_hc(h, ws):
            hc = {}
            hc["h"] = h
            hc["ws"] = ws
            hc["par"] = h % 2
            hc["g"] = h // 2
            hc["rb"] = 64 * (h % 2)
            hc["xh"] = featT.ap()[hc["rb"]:hc["rb"] + 64,
                                  2048 * hc["g"]:2048 * (hc["g"] + 1)]
            if ws == 0:
                hc["qT"], hc["kT"] = qT, kT
                hc["attnp"] = [attn, attn3.ap()]
                hc["vtok"], hc["ctxT"] = vtok, ctxT
                hc["vfm"] = vfm
            else:
                hc["qT"], hc["kT"] = qT_B.ap(), kT_B.ap()
                hc["attnp"] = [attn2.ap(), attn4.ap()]
                hc["vtok"], hc["ctxT"] = vtok_B.ap(), ctxT_Bt.ap()
                hc["vfm"] = ctxT_Bt.ap()
            hc["sq"] = hc["attnp"][0][0:64, :]
            hc["R"] = R0 if ws == 0 else R1
            hc["r2r"] = 64 * ws
            hc["r32"] = row32.ap()[0:1, :] if ws == 0 else row32.ap()[64:65, :]
            hc["rkr"] = rstdk_r.ap()[0:1, :] if ws == 0 else rstdk_r.ap()[64:65, :]
            hc["rstdb"] = rstdb if ws == 0 else attn4.ap()[0:64, :]
            hc["qsc"] = stats2.ap()[:, 0:16] if ws == 0 else stats2.ap()[:, 16:32]
            hc["o1r"] = ones1r.ap()[0:1, :] if ws == 0 else ones1r.ap()[64:65, :]
            hc["id1"] = id1f if ws == 0 else ident_f.ap()[64:65, 64:65]
            sc = statc if ws == 0 else statc_B
            hc["l"] = sc.ap()[:, 16:32]
            hc["rl"] = sc.ap()[:, 32:48]
            hc["rstdq"] = sc.ap()[:, 48:64]
            hc["statcx0"] = 256 + 16 * ws
            hc["ctx_evs"] = []
            hc["pend"] = []
            hc["attn_war"] = [None, None]
            hc["scale_tok"] = {}
            return hc

        def emit_proj_pair(hcs, prior):
            st = {id(hc): {} for hc in hcs}
            for hc in hcs:
                hc["t_krev_h"] = [None, None]
                hc["t_qrs_h"] = [None, None]
                hc["t_evr_h"] = [None, None]
            for half in range(2):
                hb = 1024 * half
                for hc in hcs:
                    s_, R, xh, rb = st[id(hc)], hc["R"], hc["xh"], hc["rb"]
                    def pe_kv(e, R=R, xh=xh, rb=rb, hb=hb):
                        for i in range(2):
                            e.matmul(R.ap()[:, 512 * i:512 * (i + 1)],
                                     Mkv.ap()[rb:rb + 64, :],
                                     xh[:, hb + 512 * i:hb + 512 * (i + 1)],
                                     start=True, stop=True, tile_position=(rb, 0))
                        return None
                    w = (list(prior) + [t_emkv]) if half == 0 else [s_["krev"]]
                    s_["kv"] = OP("tensor", pe_kv, waits=w, inc="pe")
                for hc in hcs:
                    s_, R, sqh = st[id(hc)], hc["R"], hc["sq"]
                    s_["sqk"] = OP("scalar",
                                   lambda e, R=R, sqh=sqh, hb=hb: e.activation(
                                       out=sqh[:, hb:hb + 1024], in_=R.ap()[0:64, :],
                                       func=AF.Square),
                                   waits=[s_["kv"], s_.get("ssq")], inc="act")
                for hc in hcs:
                    s_, sqh, r2r = st[id(hc)], hc["sq"], hc["r2r"]
                    def pe_ssk(e, sqh=sqh, r2r=r2r, hb=hb):
                        for i in range(2):
                            e.matmul(R2.ap()[r2r:r2r + 1, 512 * i:512 * (i + 1)],
                                     ones64.ap(), sqh[:, hb + 512 * i:hb + 512 * (i + 1)],
                                     start=True, stop=True)
                        return None
                    s_["ssk"] = OP("tensor", pe_ssk,
                                   waits=[s_["sqk"], s_.get("ln1")], inc="pe")
                for hc in hcs:
                    s_, r32, r2r = st[id(hc)], hc["r32"], hc["r2r"]
                    def act_lnk(e, r32=r32, r2r=r2r):
                        inst = None
                        for i in range(2):
                            inst = e.activation(out=r32[:, 512 * i:512 * (i + 1)],
                                                in_=R2.ap()[r2r:r2r + 1,
                                                            512 * i:512 * (i + 1)],
                                                func=AF.Ln, scale=1.0 / 64.0,
                                                bias=c_epsA.ap()[0:1, :])
                        return inst
                    s_["ln1"] = OP("scalar", act_lnk,
                                   waits=[s_["ssk"], s_.get("stc")], inc="act")
                for hc in hcs:
                    s_, r32, rkr = st[id(hc)], hc["r32"], hc["rkr"]
                    s_["ex1"] = OP("scalar",
                                   lambda e, r32=r32, rkr=rkr, hb=hb: e.activation(
                                       out=rkr[:, hb:hb + 1024], in_=r32[:, 0:1024],
                                       func=AF.Exp, scale=-0.5),
                                   waits=[s_["ln1"]], inc="act")
                for hc in hcs:
                    s_, rkr, r2r = st[id(hc)], hc["rkr"], hc["r2r"]
                    o1r = hc["o1r"]
                    def pe_bck(e, rkr=rkr, r2r=r2r, hb=hb, o1r=o1r):
                        for i in range(2):
                            e.matmul(R2.ap()[r2r:r2r + 64, 512 * i:512 * (i + 1)],
                                     o1r, rkr[:, hb + 512 * i:hb + 512 * (i + 1)],
                                     start=True, stop=True)
                        return None
                    s_["bck"] = OP("tensor", pe_bck,
                                   waits=[s_["ex1"], s_.get("cpb")], inc="pe")
                for hc in hcs:
                    s_, rsb, r2r = st[id(hc)], hc["rstdb"], hc["r2r"]
                    s_["cpb"] = OP("scalar",
                                   lambda e, rsb=rsb, r2r=r2r, hb=hb: e.activation(
                                       out=rsb[:, hb:hb + 1024],
                                       in_=R2.ap()[r2r:r2r + 64, 0:1024],
                                       func=AF.Copy),
                                   waits=[s_["bck"]], inc="act")
                for hc in hcs:
                    s_, R, kTh, rsb = st[id(hc)], hc["R"], hc["kT"], hc["rstdb"]
                    s_["kcn"] = OP("vector",
                                   lambda e, R=R, kTh=kTh, rsb=rsb, hb=hb:
                                   e.scalar_tensor_tensor(
                                       out=kTh[0:64, hb:hb + 1024],
                                       in0=R.ap()[0:64, :], scalar=wk_col.ap(),
                                       in1=rsb[:, hb:hb + 1024],
                                       op0=ALU.mult, op1=ALU.mult),
                                   waits=[s_["cpb"], t_small], inc="dve")
                for hc in hcs:
                    s_, R, vfmh = st[id(hc)], hc["R"], hc["vfm"]
                    s_["vfm"] = OP("vector",
                                   lambda e, R=R, vfmh=vfmh, hb=hb: e.tensor_copy(
                                       out=vfmh[:, hb:hb + 1024],
                                       in_=R.ap()[64:128, :]),
                                   waits=[s_["kv"]], inc="dve")
                for hc in hcs:
                    s_, vfmh, ws = st[id(hc)], hc["vfm"], hc["ws"]
                    def pe_vtr(e, vfmh=vfmh, ws=ws, hb=hb):
                        for k in range(8):
                            e.transpose(TS.ap()[:, 512 * ws + 64 * k:512 * ws + 64 * (k + 1)],
                                        vfmh[:, hb + 128 * k:hb + 128 * (k + 1)], id64b)
                        return None
                    t_vtr = OP("tensor", pe_vtr,
                               waits=[s_["vfm"]] + list(g_ev_prev[0]), inc="pe")
                    s_["vev"] = OP("vector",
                                   lambda e, hc=hc, ws=ws, hb=hb: e.tensor_copy(
                                       out=hc["vtok"][:, hb // 2:hb // 2 + 512],
                                       in_=TS.ap()[:, 512 * ws:512 * ws + 512]),
                                   waits=[t_vtr], inc="dve")
                    g_ev_prev[0] = [s_["vev"]]
                for hc in hcs:
                    s_, R, xh, rb = st[id(hc)], hc["R"], hc["xh"], hc["rb"]
                    def pe_qc(e, R=R, xh=xh, rb=rb, hb=hb):
                        for i in range(2):
                            e.matmul(R.ap()[0:64, 512 * i:512 * (i + 1)],
                                     MQt.ap()[rb:rb + 64, :],
                                     xh[:, hb + 512 * i:hb + 512 * (i + 1)],
                                     start=True, stop=True, tile_position=(rb, 0))
                        return None
                    s_["qc"] = OP("tensor", pe_qc,
                                  waits=[s_["kcn"], s_["vfm"], t_emqt], inc="pe")
                for hc in hcs:
                    s_, R, sqh = st[id(hc)], hc["R"], hc["sq"]
                    s_["sqq"] = OP("scalar",
                                   lambda e, R=R, sqh=sqh, hb=hb: e.activation(
                                       out=sqh[:, hb:hb + 1024], in_=R.ap()[0:64, :],
                                       func=AF.Square),
                                   waits=[s_["qc"], s_["ssk"]], inc="act")
                for hc in hcs:
                    s_, sqh, r2r = st[id(hc)], hc["sq"], hc["r2r"]
                    def pe_ssq(e, sqh=sqh, r2r=r2r, hb=hb):
                        for i in range(2):
                            e.matmul(R2.ap()[r2r:r2r + 1, 512 * i:512 * (i + 1)],
                                     ones64.ap(), sqh[:, hb + 512 * i:hb + 512 * (i + 1)],
                                     start=True, stop=True)
                        return None
                    s_["ssq"] = OP("tensor", pe_ssq,
                                   waits=[s_["sqq"], s_["ln1"]], inc="pe")
                for hc in hcs:
                    s_, r32, r2r = st[id(hc)], hc["r32"], hc["r2r"]
                    s_["evr"] = OP("vector",
                                   lambda e, r32=r32, r2r=r2r: e.tensor_copy(
                                       out=r32[:, 0:1024],
                                       in_=R2.ap()[r2r:r2r + 1, 0:1024]),
                                   waits=[s_["ssq"], s_["ex1"]], inc="dve")
                    hc["t_evr_h"][half] = s_["evr"]
                for hc in hcs:
                    s_, r32, stx = st[id(hc)], hc["r32"], hc["statcx0"]
                    idh = hc["id1"]
                    def pe_stc(e, r32=r32, stx=stx, half=half, idh=idh):
                        for c in range(8):
                            e.transpose(CX.ap()[:, stx + 8 * half + c:stx + 8 * half + c + 1],
                                        r32[:, 128 * c:128 * (c + 1)], idh)
                        return None
                    s_["stc"] = OP("tensor", pe_stc, waits=[s_["evr"]], inc="pe")
                for hc in hcs:
                    s_, R, qTh = st[id(hc)], hc["R"], hc["qT"]
                    s_["qev"] = OP("vector",
                                   lambda e, R=R, qTh=qTh, hb=hb: e.tensor_scalar_mul(
                                       out=qTh[0:64, hb:hb + 1024],
                                       in0=R.ap()[0:64, :], scalar1=wq_col.ap()),
                                   waits=[s_["qc"], t_wq], inc="dve")
                for hc in hcs:
                    s_, stx, qsc = st[id(hc)], hc["statcx0"], hc["qsc"]
                    t_qln = OP("scalar",
                               lambda e, stx=stx, qsc=qsc, half=half: e.activation(
                                   out=qsc[:, 8 * half:8 * half + 8],
                                   in_=CX.ap()[:, stx + 8 * half:stx + 8 * half + 8],
                                   func=AF.Ln, scale=1.0 / 64.0, bias=c_epsA.ap()),
                               waits=[s_["stc"]], inc="act")
                    s_["qrs"] = OP("scalar",
                                   lambda e, hc=hc, qsc=qsc, half=half: e.activation(
                                       out=hc["rstdq"][:, 8 * half:8 * half + 8],
                                       in_=qsc[:, 8 * half:8 * half + 8],
                                       func=AF.Exp, scale=-0.5),
                                   waits=[t_qln], inc="act")
                    hc["t_qrs_h"][half] = s_["qrs"]
                for hc in hcs:
                    s_, R, qTh = st[id(hc)], hc["R"], hc["qT"]
                    def pe_qr(e, R=R, qTh=qTh, hb=hb):
                        for i in range(2):
                            e.matmul(R.ap()[64:128, 512 * i:512 * (i + 1)],
                                     WQRt_s.ap(),
                                     qTh[0:64, hb + 512 * i:hb + 512 * (i + 1)],
                                     start=True, stop=True, tile_position=(0, 64))
                        return None
                    s_["qr"] = OP("tensor", pe_qr,
                                  waits=[s_["qev"], s_["sqq"], t_ev3], inc="pe")
                for hc in hcs:
                    s_, R, qTh = st[id(hc)], hc["R"], hc["qT"]
                    s_["qrev"] = OP("vector",
                                    lambda e, R=R, qTh=qTh, hb=hb: e.tensor_copy(
                                        out=qTh[64:128, hb:hb + 1024],
                                        in_=R.ap()[64:128, :]),
                                    waits=[s_["qr"]], inc="dve")
                for hc in hcs:
                    s_, R, xh, rb = st[id(hc)], hc["R"], hc["xh"], hc["rb"]
                    def pe_kr(e, R=R, xh=xh, rb=rb, hb=hb):
                        for i in range(2):
                            e.matmul(R.ap()[64:128, 512 * i:512 * (i + 1)],
                                     WKRt_s.ap()[rb:rb + 64, :],
                                     xh[:, hb + 512 * i:hb + 512 * (i + 1)],
                                     start=True, stop=True, tile_position=(rb, 64))
                        return None
                    s_["kr"] = OP("tensor", pe_kr, waits=[s_["qrev"]], inc="pe")
                for hc in hcs:
                    s_, R, kTh = st[id(hc)], hc["R"], hc["kT"]
                    s_["krev"] = OP("vector",
                                    lambda e, R=R, kTh=kTh, hb=hb: e.tensor_copy(
                                        out=kTh[64:128, hb:hb + 1024],
                                        in_=R.ap()[64:128, :]),
                                    waits=[s_["kr"]], inc="dve")
                    hc["t_krev_h"][half] = s_["krev"]
            g_reg_free[0] = [hcs[0]["t_krev_h"][1]]
            g_reg_free[1] = [hcs[1]["t_krev_h"][1]]
            g_reg_free[2] = [hcs[0]["t_evr_h"][1], hcs[1]["t_evr_h"][1]]

        def emit_tctx(hc):
            qtp = hc["pend"].pop(0)
            nkt = qtp + 1
            slot = 2 * hc["ws"] + (qtp % 2)
            cslot = CX.ap()[0:64, 128 * slot:128 * slot + 128]
            slot_prev = hc["ctx_evs"][-2] if len(hc["ctx_evs"]) >= 2 else None
            cev_prev = g_cev[0]
            ab = hc["attnp"][qtp % 2]
            nb = cdiv(nkt, 8)
            t_ctx = None
            tr_last = None
            for b in range(nb):
                k0, k1 = 8 * b, min(nkt, 8 * b + 8)
                def pe_tb(e, k0=k0, k1=k1, ab=ab):
                    for k in range(k0, k1):
                        e.transpose(TS.ap()[:, 128 * (k - k0):128 * (k - k0 + 1)],
                                    ab[:, 128 * k:128 * (k + 1)], id128b)
                    return None
                tr_last = OP("tensor", pe_tb,
                             waits=list(hc["scale_tok"][qtp]) + list(g_ev_prev[0]),
                             inc="pe")
                n = k1 - k0
                nh = (n + 1) // 2
                use_dve = (b % 2 == 0)
                def mkev(lo, hi, use_dve=use_dve, k0=k0):
                    def f(e, lo=lo, hi=hi, k0=k0):
                        if use_dve:
                            return e.tensor_copy(out=attnT[:, 128 * lo:128 * hi],
                                                 in_=TS.ap()[:, 128 * (lo - k0):128 * (hi - k0)])
                        return e.activation(out=attnT[:, 128 * lo:128 * hi],
                                            in_=TS.ap()[:, 128 * (lo - k0):128 * (hi - k0)],
                                            func=AF.Copy)
                    return f
                eng = "vector" if use_dve else "scalar"
                esem = "dve" if use_dve else "act"
                ev0 = OP(eng, mkev(k0, k0 + nh),
                         waits=[tr_last, g_attnT_rd[0]], inc=esem)
                if n > nh:
                    ev1 = OP(eng, mkev(k0 + nh, k1), waits=[ev0], inc=esem)
                else:
                    ev1 = None
                g_ev_prev[0] = [ev0, ev1] if ev1 else [ev0]
                def pe_ctx0(e, k0=k0, nh=nh, cslot=cslot, hc=hc, nkt=nkt):
                    for k in range(k0, k0 + nh):
                        e.matmul(cslot, hc["vtok"][:, 64 * k:64 * (k + 1)],
                                 attnT[:, 128 * k:128 * (k + 1)],
                                 start=(k == 0), stop=(k == nkt - 1))
                    return None
                w = [ev0]
                if b == 0:
                    w += [slot_prev, cev_prev]
                t_ctx = OP("tensor", pe_ctx0, waits=w, inc="pe")
                if n > nh:
                    def pe_ctx1(e, k0=k0, nh=nh, k1=k1, cslot=cslot, hc=hc, nkt=nkt):
                        for k in range(k0 + nh, k1):
                            e.matmul(cslot, hc["vtok"][:, 64 * k:64 * (k + 1)],
                                     attnT[:, 128 * k:128 * (k + 1)],
                                     start=(k == 0), stop=(k == nkt - 1))
                        return None
                    t_ctx = OP("tensor", pe_ctx1, waits=[ev1], inc="pe")
                g_attnT_rd[0] = t_ctx
            if qtp % 2 == 0:
                t_cev = OP("vector",
                           lambda e, cslot=cslot, qtp=qtp, hc=hc: e.tensor_copy(
                               out=hc["ctxT"][:, 128 * qtp:128 * (qtp + 1)], in_=cslot),
                           waits=[t_ctx], inc="dve")
            else:
                t_cev = OP("scalar",
                           lambda e, cslot=cslot, qtp=qtp, hc=hc: e.activation(
                               out=hc["ctxT"][:, 128 * qtp:128 * (qtp + 1)], in_=cslot,
                               func=AF.Copy),
                           waits=[t_ctx], inc="act")
            hc["ctx_evs"].append(t_cev)
            g_cev[0] = t_cev
            hc["attn_war"][qtp % 2] = tr_last

        def emit_qt(hc, qt):
            L = 128 * (qt + 1)
            nseg = cdiv(L, 1024)
            if len(hc["pend"]) >= 2:
                emit_tctx(hc)
            ab = hc["attnp"][qt % 2]
            rstdq = hc["rstdq"]
            l_co = hc["l"]
            hw = qt // 8
            proj_w = [hc["t_krev_h"][0]] + ([hc["t_krev_h"][1]] if hw else [])
            ev_toks = []
            for j in range(nseg):
                c0 = 1024 * j
                c1 = min(L, c0 + 1024)
                rgi = g_si[0] % 3
                g_si[0] += 1
                reg = REG[rgi]
                def pe_seg(e, reg=reg, c0=c0, c1=c1, qt=qt, qTh=hc["qT"], kTh=hc["kT"]):
                    for cc in range(c0, c1, 512):
                        ce = min(c1, cc + 512)
                        e.matmul(reg.ap()[:, cc - c0:ce - c0],
                                 qTh[:, 128 * qt:128 * (qt + 1)], kTh[:, cc:ce],
                                 start=True, stop=True)
                    return None
                t_sg = OP("tensor", pe_seg,
                          waits=proj_w + list(g_reg_free[rgi]), inc="pe")
                use_dve = (g_si[0] % 2 == 0)
                def ev_sg(e, reg=reg, c0=c0, c1=c1, ab=ab, use_dve=use_dve):
                    if use_dve:
                        return e.tensor_copy(out=ab[:, c0:c1], in_=reg.ap()[:, 0:c1 - c0])
                    return e.activation(out=ab[:, c0:c1], in_=reg.ap()[:, 0:c1 - c0],
                                        func=AF.Copy)
                t_ev = OP("vector" if use_dve else "scalar", ev_sg,
                          waits=[t_sg, hc["attn_war"][qt % 2]],
                          inc="dve" if use_dve else "act")
                g_reg_free[rgi] = [t_ev]
                ev_toks.append(t_ev)
            if hc["pend"]:
                emit_tctx(hc)
            t_mk = OP("vector",
                      lambda e, ab=ab, qt=qt: e.tensor_tensor(
                          out=ab[:, 128 * qt:128 * (qt + 1)],
                          in0=ab[:, 128 * qt:128 * (qt + 1)],
                          in1=cmask.ap(), op=ALU.add),
                      waits=[ev_toks[-1]], inc="dve")
            def act_exp(e, ab=ab, L=L, qt=qt):
                return e.activation(out=ab[:, 0:L], in_=ab[:, 0:L],
                                    func=AF.Exp, scale=rstdq[:, qt:qt + 1],
                                    accum_out=l_co[:, qt:qt + 1])
            t_ex = OP("scalar", act_exp,
                      waits=ev_toks + [t_mk, hc["t_qrs_h"][hw]], inc="act")
            hc["scale_tok"][qt] = [t_ex]
            hc["pend"].append(qt)

        def emit_wo(hc):
            while hc["pend"]:
                emit_tctx(hc)
            hc["t_rl"] = OP("vector",
                            lambda e, hc=hc: e.reciprocal(out=hc["rl"], in_=hc["l"]),
                            waits=[hc["scale_tok"][15][0]], inc="dve")
            par = hc["par"]
            def pe_wo(e, par=par, hc=hc):
                for c in range(4):
                    e.matmul(REG[c // 2].ap()[64 * par:64 * par + 64,
                                              512 * (c % 2):512 * (c % 2) + 512],
                             WOt_s.ap(), hc["ctxT"][:, 512 * c:512 * (c + 1)],
                             start=True, stop=True, tile_position=(0, 64 * par))
                return None
            t_wo = OP("tensor", pe_wo,
                      waits=[hc["ctx_evs"][-1], t_ev3]
                      + list(g_reg_free[0]) + list(g_reg_free[1]),
                      inc="pe")
            def act_ofm(e, par=par):
                e.activation(out=outFM[64 * par:64 * par + 64, 0:1024],
                             in_=R0.ap()[64 * par:64 * par + 64, :], func=AF.Copy)
                return e.activation(out=outFM[64 * par:64 * par + 64, 1024:2048],
                                    in_=R1.ap()[64 * par:64 * par + 64, :], func=AF.Copy)
            t_ofm = OP("scalar", act_ofm, waits=[t_wo], inc="act")
            g_reg_free[0] = [t_ofm]
            g_reg_free[1] = [t_ofm]
            return t_ofm

        for p in range(npairs):
            hA = mk_hc(2 * p, 0)
            hB = mk_hc(2 * p + 1, 1)
            prior = list(last_head_tokens)
            emit_proj_pair([hA, hB], prior)
            for qt in range(16):
                emit_qt(hA, qt)
                emit_qt(hB, qt)
            t_ofmA = emit_wo(hA)
            t_ofmB = emit_wo(hB)
            add_toks = []
            for bb in range(4):
                pp = bb % 2
                Rb = R0 if pp == 0 else R1
                def pe_otr(e, bb=bb, Rb=Rb):
                    for i in range(4):
                        t = 4 * bb + i
                        e.transpose(Rb.ap()[:, 128 * i:128 * (i + 1)],
                                    outFM[:, 128 * t:128 * (t + 1)], id128f)
                    return None
                w = [t_ofmB]
                if len(add_toks) >= 2:
                    w.append(add_toks[-2])
                t_tr = OP("tensor", pe_otr, waits=w, inc="pe")
                def dve_add(e, bb=bb, Rb=Rb, p=p):
                    inst = None
                    for i in range(4):
                        t = 4 * bb + i
                        tt_ = t
                        for half, hc_ in ((0, hA), (1, hB)):
                            xsl = x_tok.ap()[:, 1024 * t + 128 * p + 64 * half:
                                             1024 * t + 128 * p + 64 * (half + 1)]
                            inst = e.scalar_tensor_tensor(
                                out=xsl,
                                in0=Rb.ap()[:, 128 * i + 64 * half:128 * i + 64 * (half + 1)],
                                scalar=hc_["rl"][:, tt_:tt_ + 1], in1=xsl,
                                op0=ALU.mult, op1=ALU.add)
                    return inst
                t_add = OP("vector", dve_add,
                           waits=[t_tr, hA["t_rl"], hB["t_rl"]], inc="dve")
                add_toks.append(t_add)
                for i in range(4):
                    tok_xtok_write[4 * bb + i] = t_add
            g_reg_free[0] = [add_toks[-1]]
            g_reg_free[1] = [add_toks[-1]]
            last_head_tokens = [add_toks[-1], t_ofmB]
        ts_reader = g_ev_prev[0]

        attn_done = last_head_tokens[0]
        t_lb2 = dma("gpsimd", "gqb2", b2half,
                    b2d.rearrange("(o d) -> o d", o=1).broadcast_to([128, 1024]),
                    waits=[attn_done, last_head_tokens[1]])
        t_b2h = OP("vector", lambda e: e.tensor_scalar_mul(out=b2half, in0=b2half,
                                                           scalar1=0.5),
                   waits=[t_lb2], inc="dve")
        t_red_last = None
        t_sq_last = None
        for t in range(NT):
            t_red_last = OP("vector",
                            lambda e, t=t: e.tensor_reduce(
                                out=negmu1_c[:, t:t + 1],
                                in_=x_tok.ap()[:, 1024 * t:1024 * (t + 1)],
                                axis=AX.X, op=ALU.add, negate=True),
                            waits=[tok_xtok_write[t]], inc="dve")
            t_sq_last = OP("scalar",
                           lambda e, t=t: e.activation(
                               out=sqdump, in_=x_tok.ap()[:, 1024 * t:1024 * (t + 1)],
                               func=AF.Square, accum_out=ss1_c[:, t:t + 1]),
                           waits=[tok_xtok_write[t], attn_done, t_sq_last], inc="act")
        t_p1 = OP("vector", lambda e: e.tensor_scalar_mul(out=negmu1_c, in0=negmu1_c,
                                                          scalar1=1.0 / 1024.0),
                  waits=[t_red_last], inc="dve")
        t_p2 = OP("scalar", lambda e: e.activation(out=tmpa_c, in_=negmu1_c, func=AF.Square),
                  waits=[t_p1], inc="act")
        t_p3 = OP("vector",
                  lambda e: e.scalar_tensor_tensor(out=tmpb_c, in0=ss1_c,
                                                   scalar=1.0 / 1024.0, in1=tmpa_c,
                                                   op0=ALU.mult, op1=ALU.subtract),
                  waits=[t_p2, t_sq_last], inc="dve")
        t_p4 = OP("scalar", lambda e: e.activation(out=tmpa_c, in_=tmpb_c, func=AF.Ln,
                                                   bias=c_epsL.ap()),
                  waits=[t_p3], inc="act")
        t_p5 = OP("scalar", lambda e: e.activation(out=r1x2_c, in_=tmpa_c, func=AF.Exp,
                                                   scale=-0.5),
                  waits=[t_p4], inc="act")
        t_p6 = OP("vector", lambda e: e.tensor_scalar_add(out=tmpa_c, in0=tmpb_c,
                                                          scalar1=EPS_L),
                  waits=[t_p5], inc="dve")
        t_p7 = OP("vector", lambda e: e.reciprocal(out=tmpa_c, in_=tmpa_c),
                  waits=[t_p6], inc="dve")
        t_p8 = OP("vector", lambda e: e.tensor_tensor(out=tmpb_c, in0=tmpb_c, in1=tmpa_c,
                                                      op=ALU.mult),
                  waits=[t_p7], inc="dve")
        t_p9 = OP("scalar", lambda e: e.activation(out=tmpb_c, in_=tmpb_c, func=AF.Ln,
                                                   bias=c_epsL.ap()),
                  waits=[t_p8], inc="act")
        t_p10 = OP("scalar", lambda e: e.activation(out=tmpb_c, in_=tmpb_c, func=AF.Exp,
                                                    scale=-0.5),
                   waits=[t_p9], inc="act")
        t_p11 = OP("vector", lambda e: e.tensor_tensor(out=rho_c, in0=r1x2_c, in1=tmpb_c,
                                                       op=ALU.mult),
                   waits=[t_p10], inc="dve")
        t_p12 = OP("vector", lambda e: e.tensor_scalar_mul(out=r1x2_c, in0=r1x2_c,
                                                           scalar1=2.0),
                   waits=[t_p11], inc="dve")

        t_ev_h = None
        t_xcvt_rd = None
        for t in range(NT):
            w = [t_p11, t_xcvt_rd]
            if t == 0:
                w += last_head_tokens
            def v_h(e, t=t):
                return e.tensor_scalar(out=xcvt, in0=x_tok.ap()[:, 1024 * t:1024 * (t + 1)],
                                       scalar1=negmu1_c[:, t:t + 1],
                                       scalar2=rho_c[:, t:t + 1],
                                       op0=ALU.add, op1=ALU.mult)
            t_hc = OP("vector", v_h, waits=w, inc="dve")
            def pe_htr(e):
                for g2 in range(8):
                    e.transpose(TS.ap()[:, 128 * g2:128 * (g2 + 1)],
                                xcvt[:, 128 * g2:128 * (g2 + 1)], id128b)
                return None
            t_htr = OP("tensor", pe_htr,
                       waits=[t_hc, t_ev_h] + (ts_reader if t == 0 else []), inc="pe")
            t_xcvt_rd = t_htr
            def ev_fn2(e, t=t):
                inst = None
                for g2 in range(8):
                    inst = e.tensor_scalar(
                        out=featT.ap()[:, 2048 * g2 + 128 * t:2048 * g2 + 128 * (t + 1)],
                        in0=TS.ap()[:, 128 * g2:128 * (g2 + 1)],
                        scalar1=gcol.ap()[:, g2:g2 + 1],
                        scalar2=bcol.ap()[:, g2:g2 + 1],
                        op0=ALU.mult, op1=ALU.add)
                return inst
            t_ev_h = OP("vector", ev_fn2, waits=[t_htr, t_evrows], inc="dve")
        hT_all = [t_ev_h]


        t_silu = [None, None]
        t_zread = [None, None]
        fc2_toks = []
        slot_reader = [None] * 6
        whi_reader = [None] * 4
        fc1_toks = []
        w2_lds = []
        t_t2_done = []
        if not do_ffn:
            for t in range(NT):
                t_t2_done.append(tok_xtok_write[t] if do_attn else tok_xload[t])
        def emit_fc1(ch, ft, y_guard):
            off = (ft % 2) * 512
            wslot = ft % 4
            wbufh = whi.ap()[:, 512 * wslot:512 * (wslot + 1)]
            wl = [whi_reader[wslot]]
            if s == 0 and ch == 0 and ft < 4:
                wl.append(last_w1hi_st)
            t_w1l = dma("sync", ("w1a", "w1b", "w1c", "w1d")[wslot], wbufh,
                        w1hi[ft].rearrange("g p c -> p g c"), waits=wl)
            def pe_fc1(e, ft=ft, ch=ch, off=off, wbufh=wbufh):
                for g2 in range(4):
                    e.matmul(Bp.ap()[:, off:off + 256],
                             w1_sb.ap()[:, 4096 * g2 + 128 * ft:4096 * g2 + 128 * (ft + 1)],
                             featT.ap()[:, 2048 * g2 + TCH * ch:2048 * g2 + TCH * (ch + 1)],
                             start=(g2 == 0), stop=False)
                for g2 in range(4, NG):
                    e.matmul(Bp.ap()[:, off:off + 256],
                             wbufh[:, 128 * (g2 - 4):128 * (g2 - 3)],
                             featT.ap()[:, 2048 * g2 + TCH * ch:2048 * g2 + TCH * (ch + 1)],
                             start=False, stop=(g2 == NG - 1))
                return None
            w = [t_silu[ft % 2], t_w1l, y_guard]
            if ch == 0 and ft < 2:
                w += hT_all
            t_f1 = OP("tensor", pe_fc1, waits=w, inc="pe")
            whi_reader[wslot] = t_f1
            def act_silu(e, ft=ft, off=off):
                return e.activation(out=yT[:, TCH * ft:TCH * (ft + 1)],
                                    in_=Bp.ap()[:, off:off + 256],
                                    func=AF.Silu, bias=b1cols.ap()[:, ft:ft + 1])
            t_silu[ft % 2] = OP("scalar", act_silu, waits=[t_f1], inc="act")
            return t_silu[ft % 2]

        def emit_fc2(ch, fc, silu_tok):
            slot = fc % 6
            wbuf = w2db[:, 1024 * slot:1024 * (slot + 1)]
            wl = [st_prev[0], st_prev[1]] if (s == 0 and ch == 0 and fc < 6) else []
            if ch == 0 and fc < 6:
                wl += hT_all
            if slot_reader[slot] is not None:
                wl.append(slot_reader[slot])
            tld = dma("sync" if slot < 3 else "gpsimd", "w2s%d" % slot, wbuf,
                      w2bf[128 * fc:128 * (fc + 1), :], waits=wl)
            def pe_fc2(e, fc=fc, ch=ch, wbuf=wbuf):
                for tt in range(2):
                    zr = R0 if tt == 0 else R1
                    for hf in range(2):
                        e.matmul(zr.ap()[:, 512 * hf:512 * (hf + 1)],
                                 yT[:, TCH * fc + 128 * tt:TCH * fc + 128 * (tt + 1)],
                                 wbuf[:, 512 * hf:512 * (hf + 1)],
                                 start=(fc == 0), stop=(fc == FT - 1))
                return None
            w = [tld, silu_tok]
            if fc == 0:
                w += [t_zread[0], t_zread[1]]
                if ch == 0:
                    w.append(last_head_tokens[0])
            tok = OP("tensor", pe_fc2, waits=w, inc="pe")
            slot_reader[slot] = tok
            return tok

        def emit_epi(ch, t_fc2):
            for tt in range(2):
                t = 2 * ch + tt
                zap = (R0 if tt == 0 else R1).ap()[:, 0:1024]
                xsl = x_tok.ap()[:, 1024 * t:1024 * (t + 1)]
                t_t1 = OP("vector",
                          lambda e, t=t, xsl=xsl: e.tensor_scalar(
                              out=xsl, in0=xsl,
                              scalar1=negmu1_c[:, t:t + 1], scalar2=r1x2_c[:, t:t + 1],
                              op0=ALU.add, op1=ALU.mult),
                          waits=[t_fc2, t_p12], inc="dve")
                t_t2 = OP("vector",
                          lambda e, zap=zap, xsl=xsl: e.scalar_tensor_tensor(
                              out=xsl, in0=zap, scalar=0.5, in1=xsl,
                              op0=ALU.mult, op1=ALU.add),
                          waits=[t_t1], inc="dve")
                t_t3 = OP("vector",
                          lambda e, xsl=xsl: e.tensor_tensor(out=xsl, in0=xsl,
                                                             in1=b2half, op=ALU.add),
                          waits=[t_t2, t_b2h], inc="dve")
                t_zread[tt] = t_t3
                t_t2_done.append(t_t3)

        if do_ffn:
            prev_sil = None
            for ch in range(NCH):
                sil = [None] * FT
                fc2_last = None
                for ft in range(FT):
                    yg = None
                    if ch > 0:
                        fc2_last = emit_fc2(ch - 1, ft, prev_sil[ft])
                        yg = fc2_last
                    sil[ft] = emit_fc1(ch, ft, yg)
                fc1_done_prev[ch] = sil[FT - 1]
                if ch > 0:
                    emit_epi(ch - 1, fc2_last)
                prev_sil = sil
            fc2_last = None
            for fc in range(FT):
                fc2_last = emit_fc2(NCH - 1, fc, prev_sil[fc])
            emit_epi(NCH - 1, fc2_last)

        t_sq3 = None
        for half in range(2):
            hs = slice(8 * half, 8 * half + 8)
            t_red3 = None
            for t in range(8 * half, 8 * half + 8):
                wdep = [t_t2_done[t]]
                t_red3 = OP("vector",
                            lambda e, t=t: e.tensor_reduce(
                                out=negmu1_c[:, t:t + 1],
                                in_=x_tok.ap()[:, 1024 * t:1024 * (t + 1)],
                                axis=AX.X, op=ALU.add, negate=True),
                            waits=wdep, inc="dve")
                t_sq3 = OP("scalar",
                           lambda e, t=t: e.activation(
                               out=sqdump, in_=x_tok.ap()[:, 1024 * t:1024 * (t + 1)],
                               func=AF.Square, accum_out=ss1_c[:, t:t + 1]),
                           waits=wdep + [t_sq3], inc="act")
            t_q1 = OP("vector",
                      lambda e, hs=hs: e.tensor_scalar_mul(
                          out=negmu1_c[:, hs], in0=negmu1_c[:, hs],
                          scalar1=1.0 / 1024.0),
                      waits=[t_red3], inc="dve")
            t_q2 = OP("scalar",
                      lambda e, hs=hs: e.activation(out=tmpa_c[:, hs],
                                                    in_=negmu1_c[:, hs],
                                                    func=AF.Square),
                      waits=[t_q1], inc="act")
            t_q3 = OP("vector",
                      lambda e, hs=hs: e.scalar_tensor_tensor(
                          out=tmpb_c[:, hs], in0=ss1_c[:, hs],
                          scalar=1.0 / 1024.0, in1=tmpa_c[:, hs],
                          op0=ALU.mult, op1=ALU.subtract),
                      waits=[t_q2, t_sq3], inc="dve")
            t_q4 = OP("scalar",
                      lambda e, hs=hs: e.activation(out=tmpb_c[:, hs],
                                                    in_=tmpb_c[:, hs], func=AF.Ln,
                                                    bias=c_epsL.ap()),
                      waits=[t_q3], inc="act")
            t_q5 = OP("scalar",
                      lambda e, hs=hs: e.activation(out=tmpb_c[:, hs],
                                                    in_=tmpb_c[:, hs], func=AF.Exp,
                                                    scale=-0.5),
                      waits=[t_q4], inc="act")
            hsem = "oq" if half == 0 else "oq2"
            for t in range(8 * half, 8 * half + 8):
                t_nm = OP("vector",
                          lambda e, t=t: e.tensor_scalar(
                              out=x_tok.ap()[:, 1024 * t:1024 * (t + 1)],
                              in0=x_tok.ap()[:, 1024 * t:1024 * (t + 1)],
                              scalar1=negmu1_c[:, t:t + 1], scalar2=tmpb_c[:, t:t + 1],
                              op0=ALU.add, op1=ALU.mult),
                          waits=[t_q5], inc="dve")
                dma("gpsimd", hsem, out_hbm[s, 128 * t:128 * (t + 1), :],
                    x_tok.ap()[:, 1024 * t:1024 * (t + 1)], waits=[t_nm])
                n_store += 1
            half_store_toks[half] = (hsem, sch.cnt[hsem])

    OP("sync", lambda e: e.nop(),
       waits=[("oq", sch.cnt["oq"]), ("oq2", sch.cnt["oq2"])])

    with nc.Block() as block:
        @block.sync
        def _(e):
            sch.replay("sync", e, sems)

        @block.tensor
        def _(e):
            sch.replay("tensor", e, sems)

        @block.vector
        def _(e):
            sch.replay("vector", e, sems)

        @block.scalar
        def _(e):
            sch.replay("scalar", e, sems)

        @block.gpsimd
        def _(e):
            sch.replay("gpsimd", e, sems)

    for g in reversed(guards):
        g.__exit__(None, None, None)
    return nc


def kernel(**inputs):
    nc = build_program()
    x = np.ascontiguousarray(inputs["L0_attn_x"])
    base = {
        "W_DQ": np.ascontiguousarray(inputs["L0_attn_W_DQ"]),
        "W_UQ": np.ascontiguousarray(inputs["L0_attn_W_UQ"]),
        "W_DKV": np.ascontiguousarray(inputs["L0_attn_W_DKV"]),
        "W_UKV": np.ascontiguousarray(inputs["L0_attn_W_UKV"]),
        "W_QR": np.ascontiguousarray(inputs["L0_attn_W_QR"]),
        "W_KR": np.ascontiguousarray(inputs["L0_attn_W_KR"]),
        "W_O": np.ascontiguousarray(inputs["L0_attn_W_O"]),
        "qnw": np.ascontiguousarray(inputs["L0_attn_q_norm_w"]),
        "knw": np.ascontiguousarray(inputs["L0_attn_k_norm_w"]),
        "gam": np.ascontiguousarray(inputs["L0_ffn_ln_gamma"]),
        "bet": np.ascontiguousarray(inputs["L0_ffn_ln_beta"]),
        "q1": np.ascontiguousarray(inputs["L0_ffn_fc1_q"]),
        "s1": np.ascontiguousarray(inputs["L0_ffn_fc1_scales"]),
        "b1": np.ascontiguousarray(inputs["L0_ffn_fc1_bias"]),
        "q2": np.ascontiguousarray(inputs["L0_ffn_fc2_q"]),
        "s2": np.ascontiguousarray(inputs["L0_ffn_fc2_scales"]),
        "b2": np.ascontiguousarray(inputs["L0_ffn_fc2_bias"]),
    }
    in_maps = []
    for c in range(NCORES):
        m = dict(base)
        m["x"] = np.ascontiguousarray(x[SEQS * c:SEQS * (c + 1)])
        in_maps.append(m)
    res = run_bass_kernel_spmd(nc, in_maps, core_ids=list(range(NCORES)))
    return np.concatenate([r["out"] for r in res.results], axis=0)


# version 21 (final)
# speedup vs baseline: 1.1508x; 1.1508x over previous
import numpy as np
import concourse.bass as bass
import concourse.mybir as mybir
from concourse.bass_utils import run_bass_kernel_spmd

F32 = mybir.dt.float32
BF16 = mybir.dt.bfloat16
I32 = mybir.dt.int32
AF = mybir.ActivationFunctionType
ALU = mybir.AluOpType
AX = mybir.AxisListType

NCORES = 8
SEQS = 4
T = 2048
D = 1024
H = 16
NT = 16
NG = 8
F = 4096
FT = 32
EPS_A = 1.1920929e-07
EPS_L = 1e-5
NEG = -1e30
TCH = 256
NCH = T // TCH

SEM_OWNER = {"pe": "tensor", "dve": "vector", "act": "scalar", "pool": "gpsimd"}


class Sched:
    ENGS = ("sync", "tensor", "vector", "scalar", "gpsimd")

    def __init__(self):
        self.ops = {e: [] for e in self.ENGS}
        self.cnt = {}

    def op(self, eng, fn, waits=(), inc=None, amt=1):
        tok = None
        if inc is not None:
            self.cnt[inc] = self.cnt.get(inc, 0) + amt
            tok = (inc, self.cnt[inc])
        ws = tuple(w for w in waits if w is not None)
        self.ops[eng].append((ws, fn, inc, amt))
        return tok

    def replay(self, eng_name, eng, sems):
        hwm = {}
        rec = _Rec(eng)
        for waits, fn, inc, amt in self.ops[eng_name]:
            for s, v in waits:
                if hwm.get(s, -1) < v:
                    eng.wait_ge(sems[s], v)
                    hwm[s] = v
            rec.last = None
            fn(rec)
            if inc is not None:
                assert rec.last is not None, f"op on {eng_name} emitted nothing but incs {inc}"
                rec.last.then_inc(sems[inc], amt)


class _Rec:

    def __init__(self, e):
        self._e = e
        self.last = None

    def __getattr__(self, n):
        a = getattr(self._e, n)
        if callable(a):
            def g(*args, **kw):
                r = a(*args, **kw)
                if r is not None and hasattr(r, "then_inc"):
                    self.last = r
                return r
            return g
        return a


def cdiv(a, b):
    return (a + b - 1) // b


def build_program(seqs=SEQS, heads=H, do_attn=True, do_ffn=True):
    nc = bass.Bass("TRN2", target_bir_lowering=False, debug=False)

    x_hbm = nc.dram_tensor("x", [SEQS, T, D], F32, kind="ExternalInput").ap()
    W_DQ = nc.dram_tensor("W_DQ", [512, 64], F32, kind="ExternalInput").ap()
    W_UQ = nc.dram_tensor("W_UQ", [64, 512], F32, kind="ExternalInput").ap()
    W_DKV = nc.dram_tensor("W_DKV", [1024, 64], F32, kind="ExternalInput").ap()
    W_UKV = nc.dram_tensor("W_UKV", [128, 1024], F32, kind="ExternalInput").ap()
    W_QR = nc.dram_tensor("W_QR", [64, 64], F32, kind="ExternalInput").ap()
    W_KR = nc.dram_tensor("W_KR", [64, 64], F32, kind="ExternalInput").ap()
    W_O = nc.dram_tensor("W_O", [64, 64], F32, kind="ExternalInput").ap()
    qnw = nc.dram_tensor("qnw", [64], F32, kind="ExternalInput").ap()
    knw = nc.dram_tensor("knw", [64], F32, kind="ExternalInput").ap()
    gam = nc.dram_tensor("gam", [1024], F32, kind="ExternalInput").ap()
    bet = nc.dram_tensor("bet", [1024], F32, kind="ExternalInput").ap()
    q1d = nc.dram_tensor("q1", [1024, 4096], I32, kind="ExternalInput").ap()
    s1d = nc.dram_tensor("s1", [8, 4096], F32, kind="ExternalInput").ap()
    b1d = nc.dram_tensor("b1", [4096], F32, kind="ExternalInput").ap()
    q2d = nc.dram_tensor("q2", [4096, 1024], I32, kind="ExternalInput").ap()
    s2d = nc.dram_tensor("s2", [32, 1024], F32, kind="ExternalInput").ap()
    b2d = nc.dram_tensor("b2", [1024], F32, kind="ExternalInput").ap()
    out_hbm = nc.dram_tensor("out", [SEQS, T, D], F32, kind="ExternalOutput").ap()
    w2bf = nc.dram_tensor("w2bf", [F, D], BF16, kind="Internal").ap()
    w1hi = nc.dram_tensor("w1hi", [FT, 4, 128, 128], BF16, kind="Internal").ap()

    sch = Sched()
    guards = []

    def sb(name, shape, dtype):
        g = nc.sbuf_tensor(name, shape, dtype)
        h = g.__enter__()
        guards.append(g)
        return h

    def psum(name, shape, dtype):
        g = nc.psum_tensor(name, shape, dtype)
        h = g.__enter__()
        guards.append(g)
        return h

    ident_b = sb("ident_b", [128, 128], BF16)
    ident_f = sb("ident_f", [128, 128], F32)
    cmask = sb("cmask", [128, 128], F32)
    MQt = sb("MQt", [128, 64], BF16)
    Mkv = sb("Mkv", [128, 128], BF16)
    WQRt_s = sb("WQRt_s", [64, 64], BF16)
    WKRt_s = sb("WKRt_s", [128, 64], BF16)
    WOt_s = sb("WOt_s", [64, 64], BF16)
    wq_col = sb("wq_col", [64, 1], F32)
    wk_col = sb("wk_col", [64, 1], F32)
    ones64 = sb("ones64", [64, 1], BF16)
    ones1r = sb("ones1r", [65, 64], BF16)
    gcol = sb("gcol", [128, 8], F32)
    bcol = sb("bcol", [128, 8], F32)
    b1cols = sb("b1cols", [128, 32], F32)
    c_epsA = sb("c_epsA", [128, 1], F32)
    c_epsL = sb("c_epsL", [128, 1], F32)
    row32 = sb("row32", [65, 1024], F32)
    rstdk_r = sb("rstdk_r", [65, 2048], BF16)
    attn2 = sb("attn2", [128, 2048], BF16)
    statc = sb("statc", [128, 64], F32)
    statc_B = sb("statc_B", [128, 64], F32)
    attn3 = sb("attn3", [128, 2048], BF16)
    attn4 = sb("attn4", [128, 2048], BF16)
    statl = sb("statl", [128, 64], F32)
    stats2 = sb("stats2", [128, 48], F32)
    x_tok = sb("x_tok", [128, NT * 1024], F32)
    featT = sb("featT", [128, NG * 2048], BF16)
    w1_sb = sb("w1_sb", [128, 4 * 4096], BF16)
    whi = sb("whi", [128, 4 * 512], BF16)
    qT_B = sb("qT_B", [128, 2048], BF16)
    kT_B = sb("kT_B", [128, 2048], BF16)
    vtok_B = sb("vtok_B", [128, 1024], BF16)
    ctxT_Bt = sb("ctxT_Bt", [64, 2048], BF16)
    U = sb("U", [128, 16 * 1024 * 2 // 2], BF16)

    negm_c = statc.ap()[:, 0:16]
    l_c = statc.ap()[:, 16:32]
    rl_c = statc.ap()[:, 32:48]
    rstdq_c = statc.ap()[:, 48:64]
    negmu1_c = statl.ap()[:, 0:16]
    ss1_c = statl.ap()[:, 16:32]
    r1x2_c = statl.ap()[:, 32:48]
    rho_c = statl.ap()[:, 48:64]
    biasq_c = stats2.ap()[:, 0:16]
    tmpa_c = stats2.ap()[:, 16:32]
    tmpb_c = stats2.ap()[:, 32:48]

    Uf = U.bitcast(F32)
    qT = U.ap()[:, 0:2048]
    kT = U.ap()[:, 2048:4096]
    attn = U.ap()[:, 4096:6144]
    sqk = U.ap()[0:64, 4096:6144]
    attnT = U.ap()[:, 6144:8192]
    rstdb = U.ap()[0:64, 6144:8192]
    vtok = U.ap()[:, 8192:9216]
    ctxT = U.ap()[0:64, 9216:11264]
    vfm = U.ap()[0:64, 9216:11264]
    xcvt = U.ap()[:, 11264:12288]
    outFM = Uf.ap()[:, 6144:7168]
    outFM = Uf.ap()[:, 6144:8192]
    yT = U.ap()[:, 0:FT * TCH]
    w2db = U.ap()[:, 8192:14336]
    sqdump = U.ap()[:, 14336:15360]
    b2half = U.ap()[:, 15360:16384]
    xt_i = x_tok.bitcast(I32)
    p1_int = xt_i.ap()[:, 0:4096]
    p1_conv = x_tok.ap()[:, 4096:8192]
    p1_bc = x_tok.ap()[:, 8192:12288]
    b1_row = x_tok.ap()[0:1, 12288:16384]
    gam_row = x_tok.ap()[0:1, 8192:9216]
    bet_row = x_tok.ap()[0:1, 9216:10240]
    p2_int = xt_i.ap()[:, 0:1024]
    p2_int2 = xt_i.ap()[:, 1024:2048]
    p2_conv = x_tok.ap()[:, 4096:5120]
    p2_conv2 = x_tok.ap()[:, 5120:6144]
    p2_bc = x_tok.ap()[:, 6144:7168]
    p2_bc2 = x_tok.ap()[:, 7168:8192]
    fTf = featT.bitcast(F32)
    p2_bf = featT.ap()[:, 0:1024]
    p2_bf2 = featT.ap()[:, 1024:2048]
    wld = fTf.ap()[:, 2048:3072]
    wld2 = fTf.ap()[:, 3072:3584]
    wukv_ld = fTf.ap()[:, 3584:4608]
    wqr_f = fTf.ap()[0:64, 4608:4672]
    wkr_f = fTf.ap()[0:64, 4672:4736]
    wo_f = fTf.ap()[0:64, 4736:4800]
    wbfs = featT.ap()[:, 9600:10880]
    wuqT = featT.ap()[:, 10880:11136]
    wukvT = featT.ap()[:, 11136:12160]
    wqrT = featT.ap()[0:64, 12160:12224]
    wkrT = featT.ap()[0:64, 12224:12288]
    woT = featT.ap()[0:64, 12288:12352]

    R0 = psum("R0", [128, 1024], F32)
    R1 = psum("R1", [128, 1024], F32)
    R2 = psum("R2", [128, 1024], F32)
    TS = psum("TS", [128, 1024], BF16)
    CX = psum("CX", [128, 512], F32)
    REG = [R0, R1, R2]
    Bp = R2

    sems = {}
    for s in ("pe", "dve", "act", "pool", "dqs", "dq1", "dq2a", "dq2b",
              "gqa", "gqb", "dxl", "w2s0", "w2s1", "w2s2", "w2s3", "w2s4", "w2s5",
              "w1a", "w1b", "w1c", "w1d", "gqc", "oq", "gqb2", "oq2", "dxl2"):
        g = nc.semaphore(s)
        sems[s] = g.__enter__()
        guards.append(g)

    OP = sch.op

    def dma(eng, qsem, out, in_, waits=(), slow=False):
        def fn(e, out=out, in_=in_, slow=slow):
            if slow:
                return e.dma_start(out=out, in_=in_, allow_slow_non_contiguous=True)
            return e.dma_start(out=out, in_=in_)
        return OP(eng, fn, waits=waits, inc=qsem, amt=16)

    id64b = ident_b.ap()[0:64, 0:64]
    id128b = ident_b.ap()
    id1f = ident_f.ap()[0:1, 0:1]
    id128f = ident_f.ap()

    def v_consts(e):
        e.memset(c_epsA.ap(), EPS_A)
        e.memset(c_epsL.ap(), EPS_L)
        e.memset(ones64.ap(), 1.0)
        e.memset(ones1r.ap(), 1.0)
        e.memset(ident_b.ap(), 0.0)
        e.memset(ident_f.ap(), 0.0)
        return e.memset(cmask.ap(), 0.0)
    t_csts = OP("vector", v_consts, inc="dve")

    def g_setup(e):
        e.affine_select(out=ident_b.ap(), in_=ident_b.ap(),
                        compare_op=ALU.not_equal, fill=1.0, base=0,
                        pattern=[[-1, 128]], channel_multiplier=1)
        e.affine_select(out=ident_f.ap(), in_=ident_f.ap(),
                        compare_op=ALU.not_equal, fill=1.0, base=0,
                        pattern=[[-1, 128]], channel_multiplier=1)
        return e.affine_select(out=cmask.ap(), in_=cmask.ap(),
                               compare_op=ALU.is_ge, fill=NEG, base=0,
                               pattern=[[-1, 128]], channel_multiplier=1)
    t_setup = OP("gpsimd", g_setup, waits=[t_csts], inc="pool")

    t_ldq = dma("sync", "dqs", wld[:, 0:256].rearrange("p (c d) -> p c d", c=4),
                W_DQ.rearrange("(c p) d -> p c d", p=128))
    t_ldkv = dma("sync", "dqs", wld[:, 256:768].rearrange("p (c d) -> p c d", c=8),
                 W_DKV.rearrange("(c p) d -> p c d", p=128))
    t_luq = dma("sync", "dqs", wld2[0:64, 0:512], W_UQ)
    t_lukv = dma("sync", "dqs", wukv_ld, W_UKV)
    t_lqr = dma("sync", "dqs", wqr_f, W_QR)
    t_lkr = dma("sync", "dqs", wkr_f, W_KR)
    t_lwo = dma("sync", "dqs", wo_f, W_O)
    t_lqn = dma("sync", "dqs", wq_col.ap(), qnw.rearrange("(d o) -> d o", o=1), slow=True)
    t_lkn = dma("sync", "dqs", wk_col.ap(), knw.rearrange("(d o) -> d o", o=1), slow=True)
    t_lg = dma("sync", "dqs", gam_row, gam.rearrange("(o d) -> o d", o=1))
    t_lb = dma("sync", "dqs", bet_row, bet.rearrange("(o d) -> o d", o=1))
    t_lb1 = dma("sync", "dqs", b1_row, b1d.rearrange("(o d) -> o d", o=1))
    t_small = ("dqs", sch.cnt["dqs"])
    t_wq = OP("vector", lambda e: e.tensor_scalar_mul(out=wq_col.ap(), in0=wq_col.ap(),
                                                      scalar1=0.125),
              waits=[t_small], inc="dve")

    t_cdq = OP("vector", lambda e: e.tensor_copy(out=wbfs[:, 0:256], in_=wld[:, 0:256]),
               waits=[t_small], inc="dve")
    t_cdkv = OP("vector", lambda e: e.tensor_copy(out=wbfs[:, 256:768], in_=wld[:, 256:768]),
                waits=[t_small], inc="dve")
    t_cuq = OP("vector", lambda e: e.tensor_copy(out=wbfs[0:64, 768:1280],
                                                 in_=wld2[0:64, 0:512]),
               waits=[t_small], inc="dve")
    wukv_bf = wbfs
    wukv_bf = featT.ap()[:, 2048:3072]
    t_cukv = OP("vector", lambda e: e.tensor_copy(out=wukv_bf, in_=wukv_ld),
                waits=[t_small], inc="dve")
    wqr_b = featT.ap()[0:64, 3072:3136]
    wkr_b = featT.ap()[0:64, 3136:3200]
    wo_b = featT.ap()[0:64, 3200:3264]
    t_cqr = OP("vector", lambda e: e.tensor_copy(out=wqr_b, in_=wqr_f), waits=[t_small], inc="dve")
    t_ckr = OP("vector", lambda e: e.tensor_copy(out=wkr_b, in_=wkr_f), waits=[t_small], inc="dve")
    t_cwo = OP("vector", lambda e: e.tensor_copy(out=wo_b, in_=wo_f), waits=[t_small], inc="dve")

    def pe_tr_wuq(e):
        for c in range(4):
            e.transpose(TS.ap()[:, 64 * c:64 * (c + 1)],
                        wbfs[0:64, 768 + 128 * c:768 + 128 * (c + 1)], id64b)
        return None
    t_pe1 = OP("tensor", pe_tr_wuq, waits=[t_cuq, t_setup], inc="pe")
    t_ev1 = OP("vector", lambda e: e.tensor_copy(out=wuqT, in_=TS.ap()[:, 0:256]),
               waits=[t_pe1], inc="dve")

    def pe_tr_wukv(e):
        for c in range(8):
            e.transpose(TS.ap()[:, 128 * c:128 * (c + 1)],
                        wukv_bf[:, 128 * c:128 * (c + 1)], id128b)
        return None
    t_pe2 = OP("tensor", pe_tr_wukv, waits=[t_cukv, t_setup, t_ev1], inc="pe")
    t_ev2 = OP("vector", lambda e: e.tensor_copy(out=wukvT, in_=TS.ap()[:, 0:1024]),
               waits=[t_pe2], inc="dve")

    def pe_tr_small(e):
        e.transpose(TS.ap()[0:64, 0:64], wqr_b, id64b)
        e.transpose(TS.ap()[0:64, 64:128], wkr_b, id64b)
        e.transpose(TS.ap()[0:64, 128:192], wo_b, id64b)
        e.transpose(TS.ap()[64:128, 192:256], wkr_b, id64b, tile_position=(0, 64))
        return None
    t_pe3 = OP("tensor", pe_tr_small, waits=[t_cqr, t_ckr, t_cwo, t_ev2], inc="pe")

    def ev_small(e):
        e.tensor_copy(out=wqrT, in_=TS.ap()[0:64, 0:64])
        e.tensor_copy(out=wkrT, in_=TS.ap()[0:64, 64:128])
        e.tensor_copy(out=woT, in_=TS.ap()[0:64, 128:192])
        e.tensor_copy(out=WQRt_s.ap(), in_=TS.ap()[0:64, 0:64])
        e.tensor_copy(out=WOt_s.ap(), in_=TS.ap()[0:64, 128:192])
        e.tensor_copy(out=WKRt_s.ap()[0:64, :], in_=TS.ap()[0:64, 64:128])
        return e.tensor_copy(out=WKRt_s.ap()[64:128, :], in_=TS.ap()[64:128, 192:256])
    t_ev3 = OP("vector", ev_small, waits=[t_pe3], inc="dve")

    def pe_mqt(e):
        for tp in ((0, 0), (0, 64)):
            half = 0 if tp[1] == 0 else 1
            for c in range(4):
                e.matmul(CX.ap()[64 * half:64 * (half + 1), 0:64],
                         wbfs[:, 64 * c:64 * (c + 1)],
                         wuqT[:, 64 * c:64 * (c + 1)],
                         start=(c == 0), stop=(c == 3), tile_position=tp)
        return None
    t_mqt = OP("tensor", pe_mqt, waits=[t_cdq, t_ev1], inc="pe")
    t_emqt = OP("vector", lambda e: e.tensor_copy(out=MQt.ap(), in_=CX.ap()[:, 0:64]),
                waits=[t_mqt], inc="dve")

    def pe_mkv(e):
        for tp in ((0, 0), (0, 64)):
            half = 0 if tp[1] == 0 else 1
            for c in range(8):
                e.matmul(CX.ap()[64 * half:64 * (half + 1), 64:192],
                         wbfs[:, 256 + 64 * c:256 + 64 * (c + 1)],
                         wukvT[:, 128 * c:128 * (c + 1)],
                         start=(c == 0), stop=(c == 7), tile_position=tp)
        return None
    t_mkv = OP("tensor", pe_mkv, waits=[t_cdkv, t_ev2, t_emqt], inc="pe")
    t_emkv = OP("vector", lambda e: e.tensor_copy(out=Mkv.ap(), in_=CX.ap()[:, 64:192]),
                waits=[t_mkv], inc="dve")

    def pe_rows(e):
        for c in range(32):
            e.transpose(CX.ap()[:, 192 + c:193 + c], b1_row[:, 128 * c:128 * (c + 1)], id1f)
        for c in range(8):
            e.transpose(CX.ap()[:, 224 + c:225 + c], gam_row[:, 128 * c:128 * (c + 1)], id1f)
        for c in range(8):
            e.transpose(CX.ap()[:, 232 + c:233 + c], bet_row[:, 128 * c:128 * (c + 1)], id1f)
        return None
    t_rows = OP("tensor", pe_rows, waits=[t_small, t_setup, t_emkv], inc="pe")

    def ev_rows(e):
        e.tensor_copy(out=b1cols.ap(), in_=CX.ap()[:, 192:224])
        e.tensor_copy(out=gcol.ap(), in_=CX.ap()[:, 224:232])
        return e.tensor_copy(out=bcol.ap(), in_=CX.ap()[:, 232:240])
    t_evrows = OP("vector", ev_rows, waits=[t_rows], inc="dve")

    last_w1 = None
    last_w1hi_st = None
    t_cv_prev = None
    xt_b = x_tok.bitcast(BF16)
    w1st = xt_b.ap()[:, 24576:28672]
    for g in range(NG):
        wl = [last_w1] if last_w1 else [t_rows]
        if g == 4:
            wl.append(t_rows)
        ti = dma("sync", "dq1", p1_int, q1d[128 * g:128 * (g + 1), :], waits=wl)
        tb = dma("sync", "dq1", p1_bc, s1d[g:g + 1, :].broadcast_to([128, 4096]),
                 waits=wl + ([t_evrows] if g == 0 else []))
        t_pair = ("dq1", sch.cnt["dq1"])
        tc = OP("vector", lambda e: e.tensor_copy(out=p1_conv, in_=p1_int),
                waits=[t_pair, t_cv_prev], inc="dve")
        t_cv_prev = tc
        if g < 4:
            last_w1 = OP("vector",
                         lambda e, g=g: e.scalar_tensor_tensor(
                             out=w1_sb.ap()[:, 4096 * g:4096 * (g + 1)],
                             in0=p1_conv, scalar=-128.0, in1=p1_bc,
                             op0=ALU.add, op1=ALU.mult),
                         waits=[tc, t_pair], inc="dve")
        else:
            tdq = OP("vector",
                     lambda e: e.scalar_tensor_tensor(
                         out=w1st, in0=p1_conv, scalar=-128.0, in1=p1_bc,
                         op0=ALU.add, op1=ALU.mult),
                     waits=[tc, t_pair, last_w1hi_st], inc="dve")
            last_w1hi_st = dma("gpsimd", "gqc",
                               w1hi[:, g - 4].rearrange("f p c -> p f c"),
                               w1st.rearrange("p (f c) -> p f c", f=32),
                               waits=[tdq])
            last_w1 = tdq

    st_prev = [None, None]
    dq_prev = [None, None]
    for fc in range(FT):
        par = fc % 2
        ibuf = p2_int if par == 0 else p2_int2
        cbuf = p2_conv if par == 0 else p2_conv2
        bbuf = p2_bc if par == 0 else p2_bc2
        obuf = p2_bf if par == 0 else p2_bf2
        wl = [last_w1] if fc < 2 else []
        if st_prev[par]:
            wl.append(st_prev[par])
        dsem = "dq2a" if par == 0 else "dq2b"
        gsem = "gqa" if par == 0 else "gqb"
        ti = dma("sync", dsem, ibuf, q2d[128 * fc:128 * (fc + 1), :], waits=wl)
        tb = dma("sync", dsem, bbuf, s2d[fc:fc + 1, :].broadcast_to([128, 1024]),
                 waits=wl)
        t_pair2 = (dsem, sch.cnt[dsem])
        tc = OP("vector", lambda e, i=ibuf, c=cbuf: e.tensor_copy(out=c, in_=i),
                waits=[t_pair2, dq_prev[par]], inc="dve")
        td = OP("vector",
                lambda e, o=obuf, c=cbuf, b=bbuf: e.scalar_tensor_tensor(
                    out=o, in0=c, scalar=-128.0, in1=b, op0=ALU.add, op1=ALU.mult),
                waits=[tc], inc="dve")
        dq_prev[par] = td
        st_prev[par] = dma("gpsimd", gsem, w2bf[128 * fc:128 * (fc + 1), :], obuf,
                           waits=[td])

    prologue_done = [t for t in (last_w1, last_w1hi_st, st_prev[0], st_prev[1], t_emkv, t_ev3,
                                 t_evrows, t_csts, t_wq, t_small, t_setup) if t]

    n_store = 0
    prev_seq_done = None

    fc1_done_prev = [None] * 8
    half_store_toks = [None, None]

    for s in range(seqs):
        tok_xload = [None] * NT
        for half in range(2):
            hsem = "dxl" if half == 0 else "dxl2"
            xw = list(prologue_done) if s == 0 else [half_store_toks[half]]
            for t in range(8 * half, 8 * half + 8):
                dma("sync", hsem, x_tok.ap()[:, 1024 * t:1024 * (t + 1)],
                    x_hbm[s, 128 * t:128 * (t + 1), :], waits=xw)
                xw = []
            t_h = (hsem, sch.cnt[hsem])
            for t in range(8 * half, 8 * half + 8):
                tok_xload[t] = t_h

        t_ev_prev = None
        t_xcvt_reader = None
        for t in range(NT):
            w = [tok_xload[t], t_xcvt_reader]
            t_cv = OP("vector",
                      lambda e, t=t: e.tensor_copy(out=xcvt,
                                                   in_=x_tok.ap()[:, 1024 * t:1024 * (t + 1)]),
                      waits=w, inc="dve")
            def pe_xtr(e):
                for g2 in range(8):
                    e.transpose(TS.ap()[:, 128 * g2:128 * (g2 + 1)],
                                xcvt[:, 128 * g2:128 * (g2 + 1)], id128b)
                return None
            t_tr = OP("tensor", pe_xtr, waits=[t_cv, t_ev_prev], inc="pe")
            t_xcvt_reader = t_tr
            def ev_fn(e, t=t):
                return e.tensor_copy(
                    out=featT.ap().rearrange("p (g c) -> p g c", g=NG)[:, :, 128 * t:128 * (t + 1)],
                    in_=TS.ap().rearrange("p (g c) -> p g c", g=8))
            t_ev_prev = OP("vector", ev_fn,
                           waits=[t_tr, fc1_done_prev[t // 2]], inc="dve")
        featT_all = [t_ev_prev]

        tok_xtok_write = [None] * NT
        last_head_tokens = featT_all
        ts_reader = featT_all[0]

        npairs = (heads if do_attn else 0) // 2
        g_reg_free = [[featT_all[0]], [featT_all[0]], [featT_all[0]]]
        g_si = [0]
        g_ev_prev = [[ts_reader]]
        g_attnT_rd = [None]
        g_cev = [None]

        def mk_hc(h, ws):
            hc = {}
            hc["h"] = h
            hc["ws"] = ws
            hc["par"] = h % 2
            hc["g"] = h // 2
            hc["rb"] = 64 * (h % 2)
            hc["xh"] = featT.ap()[hc["rb"]:hc["rb"] + 64,
                                  2048 * hc["g"]:2048 * (hc["g"] + 1)]
            if ws == 0:
                hc["qT"], hc["kT"] = qT, kT
                hc["attnp"] = [attn, attn3.ap()]
                hc["vtok"], hc["ctxT"] = vtok, ctxT
                hc["vfm"] = vfm
            else:
                hc["qT"], hc["kT"] = qT_B.ap(), kT_B.ap()
                hc["attnp"] = [attn2.ap(), attn4.ap()]
                hc["vtok"], hc["ctxT"] = vtok_B.ap(), ctxT_Bt.ap()
                hc["vfm"] = ctxT_Bt.ap()
            hc["sq"] = hc["attnp"][0][0:64, :]
            hc["R"] = R0 if ws == 0 else R1
            hc["r2r"] = 64 * ws
            hc["r32"] = row32.ap()[0:1, :] if ws == 0 else row32.ap()[64:65, :]
            hc["rkr"] = rstdk_r.ap()[0:1, :] if ws == 0 else rstdk_r.ap()[64:65, :]
            hc["rstdb"] = rstdb if ws == 0 else attn4.ap()[0:64, :]
            hc["qsc"] = stats2.ap()[:, 0:16] if ws == 0 else stats2.ap()[:, 16:32]
            hc["o1r"] = ones1r.ap()[0:1, :] if ws == 0 else ones1r.ap()[64:65, :]
            hc["id1"] = id1f if ws == 0 else ident_f.ap()[64:65, 64:65]
            sc = statc if ws == 0 else statc_B
            hc["l"] = sc.ap()[:, 16:32]
            hc["rl"] = sc.ap()[:, 32:48]
            hc["rstdq"] = sc.ap()[:, 48:64]
            hc["statcx0"] = 256 + 16 * ws
            hc["ctx_evs"] = []
            hc["pend"] = []
            hc["ctxq"] = []
            hc["attn_war"] = [None, None]
            hc["scale_tok"] = {}
            return hc

        def emit_proj_pair(hcs, prior):
            st = {id(hc): {} for hc in hcs}
            for hc in hcs:
                hc["t_krev_h"] = [None, None]
                hc["t_qrs_h"] = [None, None]
                hc["t_evr_h"] = [None, None]
            for half in range(2):
                hb = 1024 * half
                for hc in hcs:
                    s_, R, xh, rb = st[id(hc)], hc["R"], hc["xh"], hc["rb"]
                    def pe_kv(e, R=R, xh=xh, rb=rb, hb=hb):
                        for i in range(2):
                            e.matmul(R.ap()[:, 512 * i:512 * (i + 1)],
                                     Mkv.ap()[rb:rb + 64, :],
                                     xh[:, hb + 512 * i:hb + 512 * (i + 1)],
                                     start=True, stop=True, tile_position=(rb, 0))
                        return None
                    w = (list(prior) + [t_emkv]) if half == 0 else [s_["krev"]]
                    s_["kv"] = OP("tensor", pe_kv, waits=w, inc="pe")
                for hc in hcs:
                    s_, R, sqh = st[id(hc)], hc["R"], hc["sq"]
                    s_["sqk"] = OP("scalar",
                                   lambda e, R=R, sqh=sqh, hb=hb: e.activation(
                                       out=sqh[:, hb:hb + 1024], in_=R.ap()[0:64, :],
                                       func=AF.Square),
                                   waits=[s_["kv"], s_.get("ssq")], inc="act")
                for hc in hcs:
                    s_, sqh, r2r = st[id(hc)], hc["sq"], hc["r2r"]
                    def pe_ssk(e, sqh=sqh, r2r=r2r, hb=hb):
                        for i in range(2):
                            e.matmul(R2.ap()[r2r:r2r + 1, 512 * i:512 * (i + 1)],
                                     ones64.ap(), sqh[:, hb + 512 * i:hb + 512 * (i + 1)],
                                     start=True, stop=True)
                        return None
                    s_["ssk"] = OP("tensor", pe_ssk,
                                   waits=[s_["sqk"], s_.get("ln1")], inc="pe")
                for hc in hcs:
                    s_, r32, r2r = st[id(hc)], hc["r32"], hc["r2r"]
                    def act_lnk(e, r32=r32, r2r=r2r):
                        inst = None
                        for i in range(2):
                            inst = e.activation(out=r32[:, 512 * i:512 * (i + 1)],
                                                in_=R2.ap()[r2r:r2r + 1,
                                                            512 * i:512 * (i + 1)],
                                                func=AF.Ln, scale=1.0 / 64.0,
                                                bias=c_epsA.ap()[0:1, :])
                        return inst
                    s_["ln1"] = OP("scalar", act_lnk,
                                   waits=[s_["ssk"], s_.get("stc")], inc="act")
                for hc in hcs:
                    s_, r32, rkr = st[id(hc)], hc["r32"], hc["rkr"]
                    s_["ex1"] = OP("scalar",
                                   lambda e, r32=r32, rkr=rkr, hb=hb: e.activation(
                                       out=rkr[:, hb:hb + 1024], in_=r32[:, 0:1024],
                                       func=AF.Exp, scale=-0.5),
                                   waits=[s_["ln1"]], inc="act")
                for hc in hcs:
                    s_, rkr, r2r = st[id(hc)], hc["rkr"], hc["r2r"]
                    o1r = hc["o1r"]
                    def pe_bck(e, rkr=rkr, r2r=r2r, hb=hb, o1r=o1r):
                        for i in range(2):
                            e.matmul(R2.ap()[r2r:r2r + 64, 512 * i:512 * (i + 1)],
                                     o1r, rkr[:, hb + 512 * i:hb + 512 * (i + 1)],
                                     start=True, stop=True)
                        return None
                    s_["bck"] = OP("tensor", pe_bck,
                                   waits=[s_["ex1"], s_.get("cpb")], inc="pe")
                for hc in hcs:
                    s_, rsb, r2r = st[id(hc)], hc["rstdb"], hc["r2r"]
                    s_["cpb"] = OP("scalar",
                                   lambda e, rsb=rsb, r2r=r2r, hb=hb: e.activation(
                                       out=rsb[:, hb:hb + 1024],
                                       in_=R2.ap()[r2r:r2r + 64, 0:1024],
                                       func=AF.Copy),
                                   waits=[s_["bck"]], inc="act")
                for hc in hcs:
                    s_, R, kTh, rsb = st[id(hc)], hc["R"], hc["kT"], hc["rstdb"]
                    s_["kcn"] = OP("vector",
                                   lambda e, R=R, kTh=kTh, rsb=rsb, hb=hb:
                                   e.scalar_tensor_tensor(
                                       out=kTh[0:64, hb:hb + 1024],
                                       in0=R.ap()[0:64, :], scalar=wk_col.ap(),
                                       in1=rsb[:, hb:hb + 1024],
                                       op0=ALU.mult, op1=ALU.mult),
                                   waits=[s_["cpb"], t_small], inc="dve")
                for hc in hcs:
                    s_, R, vfmh = st[id(hc)], hc["R"], hc["vfm"]
                    s_["vfm"] = OP("vector",
                                   lambda e, R=R, vfmh=vfmh, hb=hb: e.tensor_copy(
                                       out=vfmh[:, hb:hb + 1024],
                                       in_=R.ap()[64:128, :]),
                                   waits=[s_["kv"]], inc="dve")
                for hc in hcs:
                    s_, vfmh, ws = st[id(hc)], hc["vfm"], hc["ws"]
                    def pe_vtr(e, vfmh=vfmh, ws=ws, hb=hb):
                        for k in range(8):
                            e.transpose(TS.ap()[:, 512 * ws + 64 * k:512 * ws + 64 * (k + 1)],
                                        vfmh[:, hb + 128 * k:hb + 128 * (k + 1)], id64b)
                        return None
                    t_vtr = OP("tensor", pe_vtr,
                               waits=[s_["vfm"]] + list(g_ev_prev[0]), inc="pe")
                    s_["vev"] = OP("vector",
                                   lambda e, hc=hc, ws=ws, hb=hb: e.tensor_copy(
                                       out=hc["vtok"][:, hb // 2:hb // 2 + 512],
                                       in_=TS.ap()[:, 512 * ws:512 * ws + 512]),
                                   waits=[t_vtr], inc="dve")
                    g_ev_prev[0] = [s_["vev"]]
                for hc in hcs:
                    s_, R, xh, rb = st[id(hc)], hc["R"], hc["xh"], hc["rb"]
                    def pe_qc(e, R=R, xh=xh, rb=rb, hb=hb):
                        for i in range(2):
                            e.matmul(R.ap()[0:64, 512 * i:512 * (i + 1)],
                                     MQt.ap()[rb:rb + 64, :],
                                     xh[:, hb + 512 * i:hb + 512 * (i + 1)],
                                     start=True, stop=True, tile_position=(rb, 0))
                        return None
                    s_["qc"] = OP("tensor", pe_qc,
                                  waits=[s_["kcn"], s_["vfm"], t_emqt], inc="pe")
                for hc in hcs:
                    s_, R, sqh = st[id(hc)], hc["R"], hc["sq"]
                    s_["sqq"] = OP("scalar",
                                   lambda e, R=R, sqh=sqh, hb=hb: e.activation(
                                       out=sqh[:, hb:hb + 1024], in_=R.ap()[0:64, :],
                                       func=AF.Square),
                                   waits=[s_["qc"], s_["ssk"]], inc="act")
                for hc in hcs:
                    s_, sqh, r2r = st[id(hc)], hc["sq"], hc["r2r"]
                    def pe_ssq(e, sqh=sqh, r2r=r2r, hb=hb):
                        for i in range(2):
                            e.matmul(R2.ap()[r2r:r2r + 1, 512 * i:512 * (i + 1)],
                                     ones64.ap(), sqh[:, hb + 512 * i:hb + 512 * (i + 1)],
                                     start=True, stop=True)
                        return None
                    s_["ssq"] = OP("tensor", pe_ssq,
                                   waits=[s_["sqq"], s_["ln1"]], inc="pe")
                for hc in hcs:
                    s_, r32, r2r = st[id(hc)], hc["r32"], hc["r2r"]
                    s_["evr"] = OP("vector",
                                   lambda e, r32=r32, r2r=r2r: e.tensor_copy(
                                       out=r32[:, 0:1024],
                                       in_=R2.ap()[r2r:r2r + 1, 0:1024]),
                                   waits=[s_["ssq"], s_["ex1"]], inc="dve")
                    hc["t_evr_h"][half] = s_["evr"]
                for hc in hcs:
                    s_, r32, stx = st[id(hc)], hc["r32"], hc["statcx0"]
                    idh = hc["id1"]
                    def pe_stc(e, r32=r32, stx=stx, half=half, idh=idh):
                        for c in range(8):
                            e.transpose(CX.ap()[:, stx + 8 * half + c:stx + 8 * half + c + 1],
                                        r32[:, 128 * c:128 * (c + 1)], idh)
                        return None
                    s_["stc"] = OP("tensor", pe_stc, waits=[s_["evr"]], inc="pe")
                for hc in hcs:
                    s_, R, qTh = st[id(hc)], hc["R"], hc["qT"]
                    s_["qev"] = OP("vector",
                                   lambda e, R=R, qTh=qTh, hb=hb: e.tensor_scalar_mul(
                                       out=qTh[0:64, hb:hb + 1024],
                                       in0=R.ap()[0:64, :], scalar1=wq_col.ap()),
                                   waits=[s_["qc"], t_wq], inc="dve")
                for hc in hcs:
                    s_, stx, qsc = st[id(hc)], hc["statcx0"], hc["qsc"]
                    t_qln = OP("scalar",
                               lambda e, stx=stx, qsc=qsc, half=half: e.activation(
                                   out=qsc[:, 8 * half:8 * half + 8],
                                   in_=CX.ap()[:, stx + 8 * half:stx + 8 * half + 8],
                                   func=AF.Ln, scale=1.0 / 64.0, bias=c_epsA.ap()),
                               waits=[s_["stc"]], inc="act")
                    s_["qrs"] = OP("scalar",
                                   lambda e, hc=hc, qsc=qsc, half=half: e.activation(
                                       out=hc["rstdq"][:, 8 * half:8 * half + 8],
                                       in_=qsc[:, 8 * half:8 * half + 8],
                                       func=AF.Exp, scale=-0.5),
                                   waits=[t_qln], inc="act")
                    hc["t_qrs_h"][half] = s_["qrs"]
                for hc in hcs:
                    s_, R, qTh = st[id(hc)], hc["R"], hc["qT"]
                    def pe_qr(e, R=R, qTh=qTh, hb=hb):
                        for i in range(2):
                            e.matmul(R.ap()[64:128, 512 * i:512 * (i + 1)],
                                     WQRt_s.ap(),
                                     qTh[0:64, hb + 512 * i:hb + 512 * (i + 1)],
                                     start=True, stop=True, tile_position=(0, 64))
                        return None
                    s_["qr"] = OP("tensor", pe_qr,
                                  waits=[s_["qev"], s_["sqq"], t_ev3], inc="pe")
                for hc in hcs:
                    s_, R, qTh = st[id(hc)], hc["R"], hc["qT"]
                    s_["qrev"] = OP("vector",
                                    lambda e, R=R, qTh=qTh, hb=hb: e.tensor_copy(
                                        out=qTh[64:128, hb:hb + 1024],
                                        in_=R.ap()[64:128, :]),
                                    waits=[s_["qr"]], inc="dve")
                for hc in hcs:
                    s_, R, xh, rb = st[id(hc)], hc["R"], hc["xh"], hc["rb"]
                    def pe_kr(e, R=R, xh=xh, rb=rb, hb=hb):
                        for i in range(2):
                            e.matmul(R.ap()[64:128, 512 * i:512 * (i + 1)],
                                     WKRt_s.ap()[rb:rb + 64, :],
                                     xh[:, hb + 512 * i:hb + 512 * (i + 1)],
                                     start=True, stop=True, tile_position=(rb, 64))
                        return None
                    s_["kr"] = OP("tensor", pe_kr, waits=[s_["qrev"]], inc="pe")
                for hc in hcs:
                    s_, R, kTh = st[id(hc)], hc["R"], hc["kT"]
                    s_["krev"] = OP("vector",
                                    lambda e, R=R, kTh=kTh, hb=hb: e.tensor_copy(
                                        out=kTh[64:128, hb:hb + 1024],
                                        in_=R.ap()[64:128, :]),
                                    waits=[s_["kr"]], inc="dve")
                    hc["t_krev_h"][half] = s_["krev"]
            g_reg_free[0] = [hcs[0]["t_krev_h"][1]]
            g_reg_free[1] = [hcs[1]["t_krev_h"][1]]
            g_reg_free[2] = [hcs[0]["t_evr_h"][1], hcs[1]["t_evr_h"][1]]

        def emit_tctx_T(hc):
            if not hc["pend"]:
                return
            qtp = hc["pend"].pop(0)
            nkt = qtp + 1
            ab = hc["attnp"][qtp % 2]
            nb = cdiv(nkt, 8)
            tr_last = None
            evs = []
            for b in range(nb):
                k0, k1 = 8 * b, min(nkt, 8 * b + 8)
                def pe_tb(e, k0=k0, k1=k1, ab=ab):
                    for k in range(k0, k1):
                        e.transpose(TS.ap()[:, 128 * (k - k0):128 * (k - k0 + 1)],
                                    ab[:, 128 * k:128 * (k + 1)], id128b)
                    return None
                tr_last = OP("tensor", pe_tb,
                             waits=list(hc["scale_tok"][qtp]) + list(g_ev_prev[0]),
                             inc="pe")
                use_dve = (b % 2 == 0)
                def ev_tb(e, k0=k0, k1=k1, use_dve=use_dve):
                    n = 128 * (k1 - k0)
                    if use_dve:
                        return e.tensor_copy(out=attnT[:, 128 * k0:128 * k0 + n],
                                             in_=TS.ap()[:, 0:n])
                    return e.activation(out=attnT[:, 128 * k0:128 * k0 + n],
                                        in_=TS.ap()[:, 0:n], func=AF.Copy)
                t_ev = OP("vector" if use_dve else "scalar", ev_tb,
                          waits=[tr_last, g_attnT_rd[0]],
                          inc="dve" if use_dve else "act")
                g_ev_prev[0] = [t_ev]
                evs.append((k0, k1, t_ev))
            hc["attn_war"][qtp % 2] = tr_last
            hc["ctxq"].append((qtp, evs))

        def emit_tctx_C(hc):
            if not hc["ctxq"]:
                return
            qtp, evs = hc["ctxq"].pop(0)
            nkt = qtp + 1
            slot = 2 * hc["ws"] + (qtp % 2)
            cslot = CX.ap()[0:64, 128 * slot:128 * slot + 128]
            slot_prev = hc["ctx_evs"][-2] if len(hc["ctx_evs"]) >= 2 else None
            cev_prev = g_cev[0]
            t_ctx = None
            for bi, (k0, k1, t_ev) in enumerate(evs):
                def pe_ctx(e, k0=k0, k1=k1, cslot=cslot, nkt=nkt, hc=hc):
                    for k in range(k0, k1):
                        e.matmul(cslot, hc["vtok"][:, 64 * k:64 * (k + 1)],
                                 attnT[:, 128 * k:128 * (k + 1)],
                                 start=(k == 0), stop=(k == nkt - 1))
                    return None
                w = [t_ev]
                if bi == 0:
                    w += [slot_prev, cev_prev]
                t_ctx = OP("tensor", pe_ctx, waits=w, inc="pe")
            g_attnT_rd[0] = t_ctx
            if qtp % 2 == 0:
                t_cev = OP("vector",
                           lambda e, cslot=cslot, qtp=qtp, hc=hc: e.tensor_copy(
                               out=hc["ctxT"][:, 128 * qtp:128 * (qtp + 1)], in_=cslot),
                           waits=[t_ctx], inc="dve")
            else:
                t_cev = OP("scalar",
                           lambda e, cslot=cslot, qtp=qtp, hc=hc: e.activation(
                               out=hc["ctxT"][:, 128 * qtp:128 * (qtp + 1)], in_=cslot,
                               func=AF.Copy),
                           waits=[t_ctx], inc="act")
            hc["ctx_evs"].append(t_cev)
            g_cev[0] = t_cev

        def emit_qt(hc, qt):
            L = 128 * (qt + 1)
            nseg = cdiv(L, 1024)
            ab = hc["attnp"][qt % 2]
            rstdq = hc["rstdq"]
            l_co = hc["l"]
            hw = qt // 8
            proj_w = [hc["t_krev_h"][0]] + ([hc["t_krev_h"][1]] if hw else [])
            ev_toks = []
            for j in range(nseg):
                c0 = 1024 * j
                c1 = min(L, c0 + 1024)
                rgi = g_si[0] % 3
                g_si[0] += 1
                reg = REG[rgi]
                def pe_seg(e, reg=reg, c0=c0, c1=c1, qt=qt, qTh=hc["qT"], kTh=hc["kT"]):
                    for cc in range(c0, c1, 512):
                        ce = min(c1, cc + 512)
                        e.matmul(reg.ap()[:, cc - c0:ce - c0],
                                 qTh[:, 128 * qt:128 * (qt + 1)], kTh[:, cc:ce],
                                 start=True, stop=True)
                    return None
                t_sg = OP("tensor", pe_seg,
                          waits=proj_w + list(g_reg_free[rgi]), inc="pe")
                use_dve = (g_si[0] % 2 == 0)
                def ev_sg(e, reg=reg, c0=c0, c1=c1, ab=ab, use_dve=use_dve):
                    if use_dve:
                        return e.tensor_copy(out=ab[:, c0:c1], in_=reg.ap()[:, 0:c1 - c0])
                    return e.activation(out=ab[:, c0:c1], in_=reg.ap()[:, 0:c1 - c0],
                                        func=AF.Copy)
                t_ev = OP("vector" if use_dve else "scalar", ev_sg,
                          waits=[t_sg, hc["attn_war"][qt % 2]],
                          inc="dve" if use_dve else "act")
                g_reg_free[rgi] = [t_ev]
                ev_toks.append(t_ev)
            t_mk = OP("vector",
                      lambda e, ab=ab, qt=qt: e.tensor_tensor(
                          out=ab[:, 128 * qt:128 * (qt + 1)],
                          in0=ab[:, 128 * qt:128 * (qt + 1)],
                          in1=cmask.ap(), op=ALU.add),
                      waits=[ev_toks[-1]], inc="dve")
            def act_exp(e, ab=ab, L=L, qt=qt):
                return e.activation(out=ab[:, 0:L], in_=ab[:, 0:L],
                                    func=AF.Exp, scale=rstdq[:, qt:qt + 1],
                                    accum_out=l_co[:, qt:qt + 1])
            t_ex = OP("scalar", act_exp,
                      waits=ev_toks + [t_mk, hc["t_qrs_h"][hw]], inc="act")
            hc["scale_tok"][qt] = [t_ex]
            hc["pend"].append(qt)

        def emit_wo(hc):
            while hc["pend"]:
                emit_tctx_T(hc)
            while hc["ctxq"]:
                emit_tctx_C(hc)
            hc["t_rl"] = OP("vector",
                            lambda e, hc=hc: e.reciprocal(out=hc["rl"], in_=hc["l"]),
                            waits=[hc["scale_tok"][15][0]], inc="dve")
            par = hc["par"]
            def pe_wo(e, par=par, hc=hc):
                for c in range(4):
                    e.matmul(REG[c // 2].ap()[64 * par:64 * par + 64,
                                              512 * (c % 2):512 * (c % 2) + 512],
                             WOt_s.ap(), hc["ctxT"][:, 512 * c:512 * (c + 1)],
                             start=True, stop=True, tile_position=(0, 64 * par))
                return None
            t_wo = OP("tensor", pe_wo,
                      waits=[hc["ctx_evs"][-1], t_ev3]
                      + list(g_reg_free[0]) + list(g_reg_free[1]),
                      inc="pe")
            def act_ofm(e, par=par):
                e.activation(out=outFM[64 * par:64 * par + 64, 0:1024],
                             in_=R0.ap()[64 * par:64 * par + 64, :], func=AF.Copy)
                return e.activation(out=outFM[64 * par:64 * par + 64, 1024:2048],
                                    in_=R1.ap()[64 * par:64 * par + 64, :], func=AF.Copy)
            t_ofm = OP("scalar", act_ofm, waits=[t_wo], inc="act")
            g_reg_free[0] = [t_ofm]
            g_reg_free[1] = [t_ofm]
            return t_ofm

        for p in range(npairs):
            hA = mk_hc(2 * p, 0)
            hB = mk_hc(2 * p + 1, 1)
            prior = list(last_head_tokens)
            emit_proj_pair([hA, hB], prior)
            for qt in range(16):
                emit_qt(hA, qt)
                emit_tctx_C(hA)
                emit_tctx_T(hB)
                emit_qt(hB, qt)
                emit_tctx_C(hB)
                emit_tctx_T(hA)
            t_ofmA = emit_wo(hA)
            t_ofmB = emit_wo(hB)
            add_toks = []
            for bb in range(4):
                pp = bb % 2
                Rb = R0 if pp == 0 else R1
                def pe_otr(e, bb=bb, Rb=Rb):
                    for i in range(4):
                        t = 4 * bb + i
                        e.transpose(Rb.ap()[:, 128 * i:128 * (i + 1)],
                                    outFM[:, 128 * t:128 * (t + 1)], id128f)
                    return None
                w = [t_ofmB]
                if len(add_toks) >= 2:
                    w.append(add_toks[-2])
                t_tr = OP("tensor", pe_otr, waits=w, inc="pe")
                def dve_add(e, bb=bb, Rb=Rb, p=p):
                    inst = None
                    for i in range(4):
                        t = 4 * bb + i
                        tt_ = t
                        for half, hc_ in ((0, hA), (1, hB)):
                            xsl = x_tok.ap()[:, 1024 * t + 128 * p + 64 * half:
                                             1024 * t + 128 * p + 64 * (half + 1)]
                            inst = e.scalar_tensor_tensor(
                                out=xsl,
                                in0=Rb.ap()[:, 128 * i + 64 * half:128 * i + 64 * (half + 1)],
                                scalar=hc_["rl"][:, tt_:tt_ + 1], in1=xsl,
                                op0=ALU.mult, op1=ALU.add)
                    return inst
                t_add = OP("vector", dve_add,
                           waits=[t_tr, hA["t_rl"], hB["t_rl"]], inc="dve")
                add_toks.append(t_add)
                for i in range(4):
                    tok_xtok_write[4 * bb + i] = t_add
            g_reg_free[0] = [add_toks[-1]]
            g_reg_free[1] = [add_toks[-1]]
            last_head_tokens = [add_toks[-1], t_ofmB]
        ts_reader = g_ev_prev[0]

        attn_done = last_head_tokens[0]
        t_lb2 = dma("gpsimd", "gqb2", b2half,
                    b2d.rearrange("(o d) -> o d", o=1).broadcast_to([128, 1024]),
                    waits=[attn_done, last_head_tokens[1]])
        t_b2h = OP("vector", lambda e: e.tensor_scalar_mul(out=b2half, in0=b2half,
                                                           scalar1=0.5),
                   waits=[t_lb2], inc="dve")
        t_red_last = None
        t_sq_last = None
        for t in range(NT):
            t_red_last = OP("vector",
                            lambda e, t=t: e.tensor_reduce(
                                out=negmu1_c[:, t:t + 1],
                                in_=x_tok.ap()[:, 1024 * t:1024 * (t + 1)],
                                axis=AX.X, op=ALU.add, negate=True),
                            waits=[tok_xtok_write[t]], inc="dve")
            t_sq_last = OP("scalar",
                           lambda e, t=t: e.activation(
                               out=sqdump, in_=x_tok.ap()[:, 1024 * t:1024 * (t + 1)],
                               func=AF.Square, accum_out=ss1_c[:, t:t + 1]),
                           waits=[tok_xtok_write[t], attn_done, t_sq_last], inc="act")
        t_p1 = OP("vector", lambda e: e.tensor_scalar_mul(out=negmu1_c, in0=negmu1_c,
                                                          scalar1=1.0 / 1024.0),
                  waits=[t_red_last], inc="dve")
        t_p2 = OP("scalar", lambda e: e.activation(out=tmpa_c, in_=negmu1_c, func=AF.Square),
                  waits=[t_p1], inc="act")
        t_p3 = OP("vector",
                  lambda e: e.scalar_tensor_tensor(out=tmpb_c, in0=ss1_c,
                                                   scalar=1.0 / 1024.0, in1=tmpa_c,
                                                   op0=ALU.mult, op1=ALU.subtract),
                  waits=[t_p2, t_sq_last], inc="dve")
        t_p4 = OP("scalar", lambda e: e.activation(out=tmpa_c, in_=tmpb_c, func=AF.Ln,
                                                   bias=c_epsL.ap()),
                  waits=[t_p3], inc="act")
        t_p5 = OP("scalar", lambda e: e.activation(out=r1x2_c, in_=tmpa_c, func=AF.Exp,
                                                   scale=-0.5),
                  waits=[t_p4], inc="act")
        t_p6 = OP("vector", lambda e: e.tensor_scalar_add(out=tmpa_c, in0=tmpb_c,
                                                          scalar1=EPS_L),
                  waits=[t_p5], inc="dve")
        t_p7 = OP("vector", lambda e: e.reciprocal(out=tmpa_c, in_=tmpa_c),
                  waits=[t_p6], inc="dve")
        t_p8 = OP("vector", lambda e: e.tensor_tensor(out=tmpb_c, in0=tmpb_c, in1=tmpa_c,
                                                      op=ALU.mult),
                  waits=[t_p7], inc="dve")
        t_p9 = OP("scalar", lambda e: e.activation(out=tmpb_c, in_=tmpb_c, func=AF.Ln,
                                                   bias=c_epsL.ap()),
                  waits=[t_p8], inc="act")
        t_p10 = OP("scalar", lambda e: e.activation(out=tmpb_c, in_=tmpb_c, func=AF.Exp,
                                                    scale=-0.5),
                   waits=[t_p9], inc="act")
        t_p11 = OP("vector", lambda e: e.tensor_tensor(out=rho_c, in0=r1x2_c, in1=tmpb_c,
                                                       op=ALU.mult),
                   waits=[t_p10], inc="dve")
        t_p12 = OP("vector", lambda e: e.tensor_scalar_mul(out=r1x2_c, in0=r1x2_c,
                                                           scalar1=2.0),
                   waits=[t_p11], inc="dve")

        t_ev_h = None
        t_xcvt_rd = None
        for t in range(NT):
            w = [t_p11, t_xcvt_rd]
            if t == 0:
                w += last_head_tokens
            def v_h(e, t=t):
                return e.tensor_scalar(out=xcvt, in0=x_tok.ap()[:, 1024 * t:1024 * (t + 1)],
                                       scalar1=negmu1_c[:, t:t + 1],
                                       scalar2=rho_c[:, t:t + 1],
                                       op0=ALU.add, op1=ALU.mult)
            t_hc = OP("vector", v_h, waits=w, inc="dve")
            def pe_htr(e):
                for g2 in range(8):
                    e.transpose(TS.ap()[:, 128 * g2:128 * (g2 + 1)],
                                xcvt[:, 128 * g2:128 * (g2 + 1)], id128b)
                return None
            t_htr = OP("tensor", pe_htr,
                       waits=[t_hc, t_ev_h] + (ts_reader if t == 0 else []), inc="pe")
            t_xcvt_rd = t_htr
            def ev_fn2(e, t=t):
                inst = None
                for g2 in range(8):
                    inst = e.tensor_scalar(
                        out=featT.ap()[:, 2048 * g2 + 128 * t:2048 * g2 + 128 * (t + 1)],
                        in0=TS.ap()[:, 128 * g2:128 * (g2 + 1)],
                        scalar1=gcol.ap()[:, g2:g2 + 1],
                        scalar2=bcol.ap()[:, g2:g2 + 1],
                        op0=ALU.mult, op1=ALU.add)
                return inst
            t_ev_h = OP("vector", ev_fn2, waits=[t_htr, t_evrows], inc="dve")
        hT_all = [t_ev_h]


        t_silu = [None, None]
        t_zread = [None, None]
        fc2_toks = []
        slot_reader = [None] * 6
        whi_reader = [None] * 4
        fc1_toks = []
        w2_lds = []
        t_t2_done = []
        if not do_ffn:
            for t in range(NT):
                t_t2_done.append(tok_xtok_write[t] if do_attn else tok_xload[t])
        def emit_fc1(ch, ft, y_guard):
            off = (ft % 2) * 512
            wslot = ft % 4
            wbufh = whi.ap()[:, 512 * wslot:512 * (wslot + 1)]
            wl = [whi_reader[wslot]]
            if s == 0 and ch == 0 and ft < 4:
                wl.append(last_w1hi_st)
            t_w1l = dma("sync", ("w1a", "w1b", "w1c", "w1d")[wslot], wbufh,
                        w1hi[ft].rearrange("g p c -> p g c"), waits=wl)
            def pe_fc1(e, ft=ft, ch=ch, off=off, wbufh=wbufh):
                for g2 in range(4):
                    e.matmul(Bp.ap()[:, off:off + 256],
                             w1_sb.ap()[:, 4096 * g2 + 128 * ft:4096 * g2 + 128 * (ft + 1)],
                             featT.ap()[:, 2048 * g2 + TCH * ch:2048 * g2 + TCH * (ch + 1)],
                             start=(g2 == 0), stop=False)
                for g2 in range(4, NG):
                    e.matmul(Bp.ap()[:, off:off + 256],
                             wbufh[:, 128 * (g2 - 4):128 * (g2 - 3)],
                             featT.ap()[:, 2048 * g2 + TCH * ch:2048 * g2 + TCH * (ch + 1)],
                             start=False, stop=(g2 == NG - 1))
                return None
            w = [t_silu[ft % 2], t_w1l, y_guard]
            if ch == 0 and ft < 2:
                w += hT_all
            t_f1 = OP("tensor", pe_fc1, waits=w, inc="pe")
            whi_reader[wslot] = t_f1
            def act_silu(e, ft=ft, off=off):
                return e.activation(out=yT[:, TCH * ft:TCH * (ft + 1)],
                                    in_=Bp.ap()[:, off:off + 256],
                                    func=AF.Silu, bias=b1cols.ap()[:, ft:ft + 1])
            t_silu[ft % 2] = OP("scalar", act_silu, waits=[t_f1], inc="act")
            return t_silu[ft % 2]

        def emit_fc2(ch, fc, silu_tok):
            slot = fc % 6
            wbuf = w2db[:, 1024 * slot:1024 * (slot + 1)]
            wl = [st_prev[0], st_prev[1]] if (s == 0 and ch == 0 and fc < 6) else []
            if ch == 0 and fc < 6:
                wl += hT_all
            if slot_reader[slot] is not None:
                wl.append(slot_reader[slot])
            tld = dma("sync" if slot < 3 else "gpsimd", "w2s%d" % slot, wbuf,
                      w2bf[128 * fc:128 * (fc + 1), :], waits=wl)
            def pe_fc2(e, fc=fc, ch=ch, wbuf=wbuf):
                for tt in range(2):
                    zr = R0 if tt == 0 else R1
                    for hf in range(2):
                        e.matmul(zr.ap()[:, 512 * hf:512 * (hf + 1)],
                                 yT[:, TCH * fc + 128 * tt:TCH * fc + 128 * (tt + 1)],
                                 wbuf[:, 512 * hf:512 * (hf + 1)],
                                 start=(fc == 0), stop=(fc == FT - 1))
                return None
            w = [tld, silu_tok]
            if fc == 0:
                w += [t_zread[0], t_zread[1]]
                if ch == 0:
                    w.append(last_head_tokens[0])
            tok = OP("tensor", pe_fc2, waits=w, inc="pe")
            slot_reader[slot] = tok
            return tok

        def emit_epi(ch, t_fc2):
            for tt in range(2):
                t = 2 * ch + tt
                zap = (R0 if tt == 0 else R1).ap()[:, 0:1024]
                xsl = x_tok.ap()[:, 1024 * t:1024 * (t + 1)]
                t_t1 = OP("vector",
                          lambda e, t=t, xsl=xsl: e.tensor_scalar(
                              out=xsl, in0=xsl,
                              scalar1=negmu1_c[:, t:t + 1], scalar2=r1x2_c[:, t:t + 1],
                              op0=ALU.add, op1=ALU.mult),
                          waits=[t_fc2, t_p12], inc="dve")
                t_t2 = OP("vector",
                          lambda e, zap=zap, xsl=xsl: e.scalar_tensor_tensor(
                              out=xsl, in0=zap, scalar=0.5, in1=xsl,
                              op0=ALU.mult, op1=ALU.add),
                          waits=[t_t1], inc="dve")
                t_t3 = OP("vector",
                          lambda e, xsl=xsl: e.tensor_tensor(out=xsl, in0=xsl,
                                                             in1=b2half, op=ALU.add),
                          waits=[t_t2, t_b2h], inc="dve")
                t_zread[tt] = t_t3
                t_t2_done.append(t_t3)

        if do_ffn:
            prev_sil = None
            for ch in range(NCH):
                sil = [None] * FT
                fc2_last = None
                for ft in range(FT):
                    yg = None
                    if ch > 0:
                        fc2_last = emit_fc2(ch - 1, ft, prev_sil[ft])
                        yg = fc2_last
                    sil[ft] = emit_fc1(ch, ft, yg)
                fc1_done_prev[ch] = sil[FT - 1]
                if ch > 0:
                    emit_epi(ch - 1, fc2_last)
                prev_sil = sil
            fc2_last = None
            for fc in range(FT):
                fc2_last = emit_fc2(NCH - 1, fc, prev_sil[fc])
            emit_epi(NCH - 1, fc2_last)

        t_sq3 = None
        for half in range(2):
            hs = slice(8 * half, 8 * half + 8)
            t_red3 = None
            for t in range(8 * half, 8 * half + 8):
                wdep = [t_t2_done[t]]
                t_red3 = OP("vector",
                            lambda e, t=t: e.tensor_reduce(
                                out=negmu1_c[:, t:t + 1],
                                in_=x_tok.ap()[:, 1024 * t:1024 * (t + 1)],
                                axis=AX.X, op=ALU.add, negate=True),
                            waits=wdep, inc="dve")
                t_sq3 = OP("scalar",
                           lambda e, t=t: e.activation(
                               out=sqdump, in_=x_tok.ap()[:, 1024 * t:1024 * (t + 1)],
                               func=AF.Square, accum_out=ss1_c[:, t:t + 1]),
                           waits=wdep + [t_sq3], inc="act")
            t_q1 = OP("vector",
                      lambda e, hs=hs: e.tensor_scalar_mul(
                          out=negmu1_c[:, hs], in0=negmu1_c[:, hs],
                          scalar1=1.0 / 1024.0),
                      waits=[t_red3], inc="dve")
            t_q2 = OP("scalar",
                      lambda e, hs=hs: e.activation(out=tmpa_c[:, hs],
                                                    in_=negmu1_c[:, hs],
                                                    func=AF.Square),
                      waits=[t_q1], inc="act")
            t_q3 = OP("vector",
                      lambda e, hs=hs: e.scalar_tensor_tensor(
                          out=tmpb_c[:, hs], in0=ss1_c[:, hs],
                          scalar=1.0 / 1024.0, in1=tmpa_c[:, hs],
                          op0=ALU.mult, op1=ALU.subtract),
                      waits=[t_q2, t_sq3], inc="dve")
            t_q4 = OP("scalar",
                      lambda e, hs=hs: e.activation(out=tmpb_c[:, hs],
                                                    in_=tmpb_c[:, hs], func=AF.Ln,
                                                    bias=c_epsL.ap()),
                      waits=[t_q3], inc="act")
            t_q5 = OP("scalar",
                      lambda e, hs=hs: e.activation(out=tmpb_c[:, hs],
                                                    in_=tmpb_c[:, hs], func=AF.Exp,
                                                    scale=-0.5),
                      waits=[t_q4], inc="act")
            hsem = "oq" if half == 0 else "oq2"
            for t in range(8 * half, 8 * half + 8):
                t_nm = OP("vector",
                          lambda e, t=t: e.tensor_scalar(
                              out=x_tok.ap()[:, 1024 * t:1024 * (t + 1)],
                              in0=x_tok.ap()[:, 1024 * t:1024 * (t + 1)],
                              scalar1=negmu1_c[:, t:t + 1], scalar2=tmpb_c[:, t:t + 1],
                              op0=ALU.add, op1=ALU.mult),
                          waits=[t_q5], inc="dve")
                dma("gpsimd", hsem, out_hbm[s, 128 * t:128 * (t + 1), :],
                    x_tok.ap()[:, 1024 * t:1024 * (t + 1)], waits=[t_nm])
                n_store += 1
            half_store_toks[half] = (hsem, sch.cnt[hsem])

    OP("sync", lambda e: e.nop(),
       waits=[("oq", sch.cnt["oq"]), ("oq2", sch.cnt["oq2"])])

    with nc.Block() as block:
        @block.sync
        def _(e):
            sch.replay("sync", e, sems)

        @block.tensor
        def _(e):
            sch.replay("tensor", e, sems)

        @block.vector
        def _(e):
            sch.replay("vector", e, sems)

        @block.scalar
        def _(e):
            sch.replay("scalar", e, sems)

        @block.gpsimd
        def _(e):
            sch.replay("gpsimd", e, sems)

    for g in reversed(guards):
        g.__exit__(None, None, None)
    return nc


def kernel(**inputs):
    nc = build_program()
    x = np.ascontiguousarray(inputs["L0_attn_x"])
    base = {
        "W_DQ": np.ascontiguousarray(inputs["L0_attn_W_DQ"]),
        "W_UQ": np.ascontiguousarray(inputs["L0_attn_W_UQ"]),
        "W_DKV": np.ascontiguousarray(inputs["L0_attn_W_DKV"]),
        "W_UKV": np.ascontiguousarray(inputs["L0_attn_W_UKV"]),
        "W_QR": np.ascontiguousarray(inputs["L0_attn_W_QR"]),
        "W_KR": np.ascontiguousarray(inputs["L0_attn_W_KR"]),
        "W_O": np.ascontiguousarray(inputs["L0_attn_W_O"]),
        "qnw": np.ascontiguousarray(inputs["L0_attn_q_norm_w"]),
        "knw": np.ascontiguousarray(inputs["L0_attn_k_norm_w"]),
        "gam": np.ascontiguousarray(inputs["L0_ffn_ln_gamma"]),
        "bet": np.ascontiguousarray(inputs["L0_ffn_ln_beta"]),
        "q1": np.ascontiguousarray(inputs["L0_ffn_fc1_q"]),
        "s1": np.ascontiguousarray(inputs["L0_ffn_fc1_scales"]),
        "b1": np.ascontiguousarray(inputs["L0_ffn_fc1_bias"]),
        "q2": np.ascontiguousarray(inputs["L0_ffn_fc2_q"]),
        "s2": np.ascontiguousarray(inputs["L0_ffn_fc2_scales"]),
        "b2": np.ascontiguousarray(inputs["L0_ffn_fc2_bias"]),
    }
    in_maps = []
    for c in range(NCORES):
        m = dict(base)
        m["x"] = np.ascontiguousarray(x[SEQS * c:SEQS * (c + 1)])
        in_maps.append(m)
    res = run_bass_kernel_spmd(nc, in_maps, core_ids=list(range(NCORES)))
    return np.concatenate([r["out"] for r in res.results], axis=0)
